# version 05
# baseline (speedup 1.0000x reference)
import numpy as np
import ml_dtypes
from contextlib import ExitStack
import concourse.bass as bass
import concourse.mybir as mybir
from concourse.bass_utils import run_bass_kernel_spmd

F32 = mybir.dt.float32
BF16 = mybir.dt.bfloat16
AF = mybir.ActivationFunctionType
ALU = mybir.AluOpType
AX = mybir.AxisListType
BF16NP = ml_dtypes.bfloat16

T = 4096
D = 1024
NH = 16
R = 32
I = 2048
E = 8
NTB = T // 128
NQT = T // 512
NCH = D // 128
NQ = 4
ENGS = ("sync", "gpsimd", "vector", "scalar", "tensor")


class Sched:
    def __init__(self, sems):
        self.sems = sems
        self.q = {e: [] for e in ENGS}
        self.cnt = {k: 0 for k in sems}
        self.lastwait = {}

    def emit(self, eng, fn):
        self.q[eng].append(fn)

    def wait(self, eng, sem, val=None):
        if val is None:
            val = self.cnt[sem]
        if val is None or val <= 0:
            return
        key = (eng, sem)
        if self.lastwait.get(key, -1) >= val:
            return
        self.lastwait[key] = val
        h = self.sems[sem]
        self.q[eng].append(lambda E, h=h, v=val: E.wait_ge(h, v))

    def inc(self, sem, n=1):
        self.cnt[sem] += n
        return self.cnt[sem]

    def tag(self, eng, sem, n=1):
        fn = self.q[eng].pop()
        h = self.sems[sem]
        self.q[eng].append(lambda E, f=fn, h=h, n=n: f(E).then_inc(h, n))
        return self.inc(sem, n)

    def tagwait(self, eng, sem):
        v = self.tag(eng, sem)
        self.wait(eng, sem, v)
        return v

    def dma(self, eng, dst, src, sem):
        h = self.sems[sem]
        self.q[eng].append(lambda E, d=dst, s=src, h=h: E.dma_start(out=d, in_=s).then_inc(h, 16))
        return self.inc(sem, 16)


def build_program(dbg=(), stop_after=None):
    nc = bass.Bass("TRN2", target_bir_lowering=False)

    x_t = nc.dram_tensor("x_t", [T, D], F32, kind="ExternalInput").ap()
    caT = nc.dram_tensor("caT", [3 * D, D], F32, kind="ExternalInput").ap()
    b_mat = nc.dram_tensor("b_mat", [64, 48], F32, kind="ExternalInput").ap()
    wkeTh = nc.dram_tensor("wkeTh", [64, 32], F32, kind="ExternalInput").ap()
    wkch = nc.dram_tensor("wkch", [64, 32], F32, kind="ExternalInput").ap()
    wvch = nc.dram_tensor("wvch", [64, 32], F32, kind="ExternalInput").ap()
    wveTh = nc.dram_tensor("wveTh", [64, 32], F32, kind="ExternalInput").ap()
    cpw = nc.dram_tensor("cpw", [D, D], F32, kind="ExternalInput").ap()
    cpb = nc.dram_tensor("cpb", [1, D], F32, kind="ExternalInput").ap()
    rw_pk = nc.dram_tensor("rw_pk", [128, 64], F32, kind="ExternalInput").ap()
    rb = nc.dram_tensor("rb", [1, E], F32, kind="ExternalInput").ap()
    sgT = nc.dram_tensor("sgT", [D, I], F32, kind="ExternalInput").ap()
    suT = nc.dram_tensor("suT", [D, I], F32, kind="ExternalInput").ap()
    sdT = nc.dram_tensor("sdT", [I, D], F32, kind="ExternalInput").ap()
    gw = nc.dram_tensor("gw", [E, D, I], F32, kind="ExternalInput").ap()
    uw = nc.dram_tensor("uw", [E, D, I], F32, kind="ExternalInput").ap()
    dw = nc.dram_tensor("dw", [E, I, D], F32, kind="ExternalInput").ap()
    identbf_h = nc.dram_tensor("identbf", [128, 128], BF16, kind="ExternalInput").ap()
    identf_h = nc.dram_tensor("identf", [128, 128], F32, kind="ExternalInput").ap()
    trimask_h = nc.dram_tensor("trimask", [128, 128], F32, kind="ExternalInput").ap()
    eg_h = nc.dram_tensor("eg", [128, 4], BF16, kind="ExternalInput").ap()

    out = nc.dram_tensor("out", [T, D], F32, kind="ExternalOutput").ap()
    x2s = nc.dram_tensor("x2s", [T, D], F32, kind="Internal").ap()
    xn2s = nc.dram_tensor("xn2s", [NTB, 128, D], BF16, kind="Internal").ap()

    dbg_t = {}
    for name, shape, dt in (
        ("d_qp", [128, 4 * T], BF16), ("d_kp", [128, 4 * T], BF16), ("d_vt", [128, 32 * 528], BF16),
        ("d_xn", [128, 8 * 512], BF16), ("d_x2", [T, D], F32), ("d_xn2", [NTB, 128, D], BF16),
        ("d_w", [128, NTB * 8], F32), ("d_at", [4, 128, 512], BF16), ("d_negc", [128, 1], F32),
        ("d_xnt", [128, D], BF16), ("d_cols", [128, 8], F32), ("d_af", [128, 2048], BF16),
    ):
        if name in dbg:
            dbg_t[name] = nc.dram_tensor(name, shape, dt, kind="ExternalOutput").ap()

    with ExitStack() as stack:
        SEMNAMES = ("lw0", "gio", "lfA", "lfB", "lcA", "lcB", "lxA", "lxB", "lrA", "lrB",
                    "sx2", "sxn", "lq", "wgA", "wgB", "wuA", "wuB", "wdA", "wdB",
                    "st", "pe", "dve", "act")
        sems = {n: stack.enter_context(nc.semaphore(n)) for n in SEMNAMES}
        S = Sched(sems)

        def SB(st_, name, shape, dt):
            return st_.enter_context(nc.sbuf_tensor(name, shape, dt))

        identbf = SB(stack, "identbf_s", [128, 128], BF16)
        identf = SB(stack, "identf_s", [128, 128], F32)
        trimask = SB(stack, "trimask_s", [128, 128], F32)
        eg = SB(stack, "eg_s", [128, 4], BF16)
        ones_b = SB(stack, "ones_b", [1, 128], BF16)
        ones_f = SB(stack, "ones_f", [1, 128], F32)
        w_tok = SB(stack, "w_tok", [128, NTB * 8], F32)
        negc_col = SB(stack, "negc_col", [128, 1], F32)
        smallA = SB(stack, "smallA", [128, 16], F32)
        rowA = SB(stack, "rowA", [1, 64], F32)
        rW = SB(stack, "rW", [128, 64], BF16)
        rb_row = SB(stack, "rb_row", [1, E], BF16)
        jcol = SB(stack, "jcol", [128, 1], F32)

        PS = stack.enter_context(nc.psum_tensor("PS", [128, 3584], F32))
        PSB = stack.enter_context(nc.psum_tensor("PSB", [128, 1024], BF16))

        def bank(i):
            return PS[:, 512 * i: 512 * (i + 1)]

        st12 = ExitStack()
        qp = [SB(st12, f"qp{g}", [128, T], BF16) for g in range(4)]
        kp = [SB(st12, f"kp{g}", [128, T], BF16) for g in range(4)]
        vT = SB(st12, "vT", [128, 32 * 528], BF16)
        Wqk = SB(st12, "Wqk", [128, 8 * 1024], BF16)
        Wv = SB(st12, "Wv", [128, 8 * 512], BF16)
        Cp = SB(st12, "Cp", [128, 4 * 1024], BF16)
        bqk_col = SB(st12, "bqk_col", [128, 8], F32)
        bv_rowf = SB(st12, "bv_rowf", [1, 512], F32)
        bv_row = SB(st12, "bv_row", [1, 512], BF16)
        cpb_rowf = SB(st12, "cpb_rowf", [1, D], F32)
        cpb_row = SB(st12, "cpb_row", [1, D], BF16)
        wk_s = SB(st12, "wk_s", [64, 32 * 4], F32)
        wkeT_s = SB(st12, "wkeT_s", [64, 32], F32)
        bmat_s = SB(st12, "bmat_s", [64, 48], F32)
        xt = [SB(st12, f"xt{i}", [128, D], F32) for i in range(2)]
        xr = [SB(st12, f"xr{i}", [128, D], F32) for i in range(2)]
        wfold = [xr[0][0:64, :], xr[1][0:64, :]]
        cpslc = [xt[0][0:64, :], xt[1][0:64, :]]
        tmp_f = SB(st12, "tmp_f", [128, D], F32)
        s12 = SB(st12, "s12", [128, 8], F32)
        mcol = SB(st12, "mcol", [128, 1], F32)
        msq = SB(st12, "msq", [128, 1], F32)
        vcol = SB(st12, "vcol", [128, 1], F32)
        lncol = SB(st12, "lncol", [128, 1], F32)
        rstd = SB(st12, "rstd", [128, 1], F32)
        xn_t = SB(st12, "xn_t", [128, D], BF16)
        xn_f = SB(st12, "xn_f", [128, 8 * 512], BF16)
        gmax2 = SB(st12, "gmax2", [4, 64], F32)
        probsT = [SB(st12, f"probsT{i}", [128, 2048], BF16) for i in range(2)]
        aTb = [SB(st12, "aT0", [128, 4 * 512], BF16), SB(st12, "aT1", [128, 4 * 512], BF16)]
        aF = SB(st12, "aF", [128, 2048], BF16)
        rlcol = SB(st12, "rlcol", [128, 16], F32)
        scoresb = [xt[0][:], xt[1][:]]
        x2_t = SB(st12, "x2_t", [128, D], F32)
        xn2_t = SB(st12, "xn2_t", [128, D], BF16)
        xn2_f = SB(st12, "xn2_f", [128, D], BF16)
        pr8 = SB(st12, "pr8", [128, E], F32)
        pr8b = SB(st12, "pr8b", [128, E], F32)
        top_a = SB(st12, "top_a", [128, E], F32)
        top_b = SB(st12, "top_b", [128, E], F32)

        for dst, src in ((identbf[:], identbf_h), (identf[:], identf_h), (trimask[:], trimask_h),
                         (eg[:], eg_h), (wk_s[:, 0:32], wkeTh), (wk_s[:, 32:64], wkch),
                         (wk_s[:, 64:96], wvch), (wk_s[:, 96:128], wveTh), (bmat_s[:], b_mat),
                         (cpb_rowf[:], cpb)):
            S.dma("sync", dst, src, "lw0")
        S.dma("gpsimd", rW[:], rw_pk, "gio")
        S.dma("gpsimd", rb_row[:], rb, "gio")

        S.emit("vector", lambda E: E.memset(ones_b[:], 1.0))
        S.emit("vector", lambda E: E.memset(ones_f[:], 1.0))
        S.emit("vector", lambda E: E.memset(vT[:], 1.0))
        S.emit("vector", lambda E: E.memset(s12[:], 0.0))
        S.emit("vector", lambda E: E.memset(gmax2[:], 0.0).then_inc(sems["dve"], 1))
        S.inc("dve")

        S.wait("scalar", "lw0")
        S.emit("scalar", lambda E: E.mul(out=wkeT_s[:], in_=wk_s[:, 0:32], mul=0.125))
        S.emit("scalar", lambda E: E.copy(out=cpb_row[:], in_=cpb_rowf[:]).then_inc(sems["act"], 1))
        S.inc("act")
        act_ph0 = S.cnt["act"]

        fold_jobs = [(p, h) for p in range(3) for h in range(NH)]
        if stop_after == "ph0":
            fold_jobs = []
        elif stop_after == "fold1":
            fold_jobs = fold_jobs[:1]
        elif stop_after == "fold3":
            fold_jobs = fold_jobs[:3]
        elif stop_after == "foldq":
            fold_jobs = fold_jobs[:16]
        elif stop_after == "foldqkv":
            pass
        elif stop_after == "foldcp":
            fold_jobs = []
        lf = ("lfA", "lfB")

        def fold_load(j):
            p, h = fold_jobs[j]
            S.dma("sync", wfold[j % 2], caT[p * D + 64 * h: p * D + 64 * h + 64, :], lf[j % 2])

        if fold_jobs:
            fold_load(0)
        if len(fold_jobs) > 1:
            fold_load(1)
        S.wait("tensor", "lw0")
        S.wait("tensor", "act", act_ph0)
        rhs_map = {0: wkeT_s[:], 1: wk_s[:, 32:64], 2: wk_s[:, 64:96]}
        for j, (p, h) in enumerate(fold_jobs):
            bufi = j % 2
            g, hp = h // 4, h % 4
            S.wait("tensor", lf[bufi], S.cnt[lf[bufi]])
            rhs = rhs_map[p]
            S.wait("tensor", "dve")
            for ch in range(NCH):
                S.emit("tensor", lambda E, b=bufi, c=ch, r=rhs: E.matmul(
                    out=PS[:, 512 * 6 + 32 * c: 512 * 6 + 32 * c + 32],
                    lhsT=wfold[b][:, 128 * c: 128 * (c + 1)],
                    rhs=r, start=True, stop=True, skip_group_check=True))
            S.tag("tensor", "pe")
            if p == 0:
                dst = Wqk[:].rearrange("q (c f) -> q c f", c=8)[:, :, g * 128 + 32 * hp: g * 128 + 32 * hp + 32]
            elif p == 1:
                dst = Wqk[:].rearrange("q (c f) -> q c f", c=8)[:, :, 512 + g * 128 + 32 * hp: 512 + g * 128 + 32 * hp + 32]
            else:
                dst = Wv[:].rearrange("q (c f) -> q c f", c=8)[:, :, 32 * h: 32 * h + 32]
            psrc = PS[:, 512 * 6: 512 * 6 + 256].rearrange("q (c f) -> q c f", c=8)
            S.wait("vector", "pe")
            S.emit("vector", lambda E, d=dst, s_=psrc: E.tensor_copy(out=d, in_=s_).then_inc(sems["dve"], 1))
            S.inc("dve")
            S.wait("tensor", "dve")
            S.emit("tensor", lambda E, pp=p, hh=h, r=rhs: E.matmul(
                out=PS[0:1, 512 * 6 + 288: 512 * 6 + 320], lhsT=bmat_s[:, pp * 16 + hh: pp * 16 + hh + 1],
                rhs=r, start=True, stop=True).then_inc(sems["pe"], 1))
            S.inc("pe")
            S.wait("vector", "pe")
            if p == 2:
                S.emit("vector", lambda E, hh=h: E.tensor_copy(
                    out=bv_rowf[0:1, 32 * hh: 32 * hh + 32],
                    in_=PS[0:1, 512 * 6 + 288: 512 * 6 + 320]).then_inc(sems["dve"], 1))
                S.inc("dve")
            else:
                S.emit("vector", lambda E: E.tensor_copy(
                    out=rowA[0:1, 0:32], in_=PS[0:1, 512 * 6 + 288: 512 * 6 + 320]).then_inc(sems["dve"], 1))
                S.inc("dve")
                S.wait("tensor", "dve")
                S.emit("tensor", lambda E: E.matmul(
                    out=PS[0:32, 512 * 6 + 320: 512 * 6 + 321], lhsT=rowA[0:1, 0:32],
                    rhs=ones_f[0:1, 0:1], start=True, stop=True).then_inc(sems["pe"], 1))
                S.inc("pe")
                S.wait("vector", "pe")
                col = g if p == 0 else 4 + g
                S.emit("vector", lambda E, c=col, hp_=hp: E.tensor_copy(
                    out=bqk_col[32 * hp_: 32 * hp_ + 32, c: c + 1],
                    in_=PS[0:32, 512 * 6 + 320: 512 * 6 + 321]).then_inc(sems["dve"], 1))
                S.inc("dve")
            S.wait("tensor", "dve")
            if j + 2 < len(fold_jobs):
                S.wait("sync", "pe")
                fold_load(j + 2)

        S.wait("scalar", "dve")
        S.emit("scalar", lambda E: E.copy(out=bv_row[:], in_=bv_rowf[:]).then_inc(sems["act"], 1))
        S.inc("act")

        lc = ("lcA", "lcB")

        def cp_load(h):
            S.dma("sync", cpslc[h % 2], cpw[64 * h: 64 * h + 64, :], lc[h % 2])

        NH_cp = 0 if stop_after in ("ph0", "fold1", "fold3", "foldq", "foldqkv") else NH
        if NH_cp:
            cp_load(0)
            cp_load(1)
        for h in range(NH_cp):
            bufi = h % 2
            g, hp = h // 4, h % 4
            S.wait("tensor", lc[bufi], S.cnt[lc[bufi]])
            for half in range(2):
                S.emit("tensor", lambda E, b=bufi, hf=half: E.matmul(
                    out=PS[0:32, 512 * hf: 512 * hf + 512], lhsT=wk_s[:, 96:128],
                    rhs=cpslc[b][:, 512 * hf: 512 * (hf + 1)], start=True, stop=True))
            S.tag("tensor", "pe")
            S.wait("vector", "pe")
            S.emit("vector", lambda E, g_=g, hp_=hp: E.tensor_copy(
                out=Cp[32 * hp_: 32 * hp_ + 32, g_ * 1024: g_ * 1024 + 1024],
                in_=PS[0:32, 0:1024]).then_inc(sems["dve"], 1))
            S.inc("dve")
            S.wait("tensor", "dve")
            if h + 2 < NH:
                S.wait("sync", "pe")
                cp_load(h + 2)

        stop_now = stop_after in ("fold", "ph0", "fold1", "fold3", "foldq", "foldqkv", "foldcp")
        lx = ("lxA", "lxB")
        xt_free_dve = {}
        s12_free = {}
        if stop_now:
            NQT_eff = 0
        else:
            NQT_eff = NQT

        def x_load(tb):
            if tb - 2 in xt_free_dve:
                S.wait("sync", "dve", xt_free_dve[tb - 2])
            S.dma("sync", xt[tb % 2][:], x_t[128 * tb: 128 * (tb + 1), :], lx[tb % 2])

        if not stop_now:
            S.wait("sync", "pe")
            S.wait("sync", "dve")
            x_load(0)
            x_load(1)
        for tile in range(NQT_eff):
            for j in range(4):
                tb = tile * 4 + j
                bufi = tb % 2
                S.wait("vector", lx[bufi], S.cnt[lx[bufi]])
                S.emit("vector", lambda E, b=bufi: E.reduce_sum(
                    out=s12[:, 0:1], in_=xt[b][:], axis=AX.X))
                S.tagwait("vector", "dve")
                S.wait("scalar", lx[bufi], S.cnt[lx[bufi]])
                S.wait("scalar", "dve", s12_free.get(tb - 1, 0))
                S.emit("scalar", lambda E, b=bufi: E.activation(
                    out=probsT[0][:, 1024:2048], in_=xt[b][:], func=AF.Square,
                    accum_out=s12[:, 4:5]).then_inc(sems["act"], 1))
                S.inc("act")
                S.wait("vector", "act")
                S.emit("vector", lambda E: E.tensor_scalar(
                    out=mcol[:], in0=s12[:, 0:1], scalar1=1.0 / D, scalar2=None, op0=ALU.mult))
                S.emit("vector", lambda E: E.tensor_scalar(
                    out=vcol[:], in0=s12[:, 4:5], scalar1=1.0 / D, scalar2=1e-5, op0=ALU.mult, op1=ALU.add))
                S.tagwait("vector", "dve")
                s12_free[tb] = S.cnt["dve"]
                S.emit("vector", lambda E: E.tensor_tensor(out=msq[:], in0=mcol[:], in1=mcol[:], op=ALU.mult))
                S.emit("vector", lambda E, b=bufi: E.tensor_scalar(
                    out=tmp_f[:], in0=xt[b][:], scalar1=mcol[:, 0:1], scalar2=None, op0=ALU.subtract))
                S.tagwait("vector", "dve")
                S.emit("vector", lambda E: E.tensor_tensor(
                    out=vcol[:], in0=vcol[:], in1=msq[:], op=ALU.subtract).then_inc(sems["dve"], 1))
                S.inc("dve")
                xt_free_dve[tb] = S.cnt["dve"]
                S.wait("scalar", "dve")
                S.emit("scalar", lambda E: E.activation(out=lncol[:], in_=vcol[:], func=AF.Ln))
                S.emit("scalar", lambda E: E.activation(
                    out=rstd[:], in_=lncol[:], func=AF.Exp, scale=-0.5).then_inc(sems["act"], 1))
                S.inc("act")
                S.wait("vector", "act")
                S.emit("vector", lambda E: E.tensor_scalar(
                    out=xn_t[:], in0=tmp_f[:], scalar1=rstd[:, 0:1], scalar2=None, op0=ALU.mult).then_inc(sems["dve"], 1))
                S.inc("dve")
                if tb == NTB - 1 and "d_xnt" in dbg_t:
                    S.emit("vector", lambda E: E.tensor_copy(out=smallA[:, 0:1], in_=mcol[:]))
                    S.emit("vector", lambda E: E.tensor_copy(out=smallA[:, 1:2], in_=rstd[:]))
                    S.emit("vector", lambda E: E.tensor_copy(out=smallA[:, 2:3], in_=s12[:, 0:1]))
                    S.emit("vector", lambda E: E.tensor_copy(out=smallA[:, 3:4], in_=s12[:, 4:5]).then_inc(sems["dve"], 1))
                    S.inc("dve")
                    S.wait("sync", "dve")
                    S.dma("sync", dbg_t["d_xnt"], xn_t[:], "st")
                    S.dma("sync", dbg_t["d_cols"], smallA[:, 0:8], "st")
                if tb + 2 < NTB:
                    x_load(tb + 2)
                S.wait("tensor", "dve")
                for ch in range(NCH):
                    S.emit("tensor", lambda E, c=ch: E.transpose(
                        out=PSB[:, 128 * c: 128 * (c + 1)], in_=xn_t[:, 128 * c: 128 * (c + 1)],
                        identity=identbf[:]))
                S.tag("tensor", "pe")
                S.wait("vector", "pe")
                xnf_dst = xn_f[:].rearrange("p (c q t) -> p c q t", c=8, q=4, t=128)[:, :, j, :]
                psb_src = PSB[:].rearrange("p (c t) -> p c t", c=8)
                S.emit("vector", lambda E, d=xnf_dst, s=psb_src: E.tensor_copy(out=d, in_=s).then_inc(sems["dve"], 1))
                S.inc("dve")
                S.wait("tensor", "dve")
                for ch in range(NCH):
                    S.emit("tensor", lambda E, c=ch, jj=j: E.matmul(
                        out=bank(2), lhsT=xn_f[:, c * 512 + 128 * jj: c * 512 + 128 * jj + 128],
                        rhs=Wv[:, c * 512: (c + 1) * 512], start=(c == 0), stop=False))
                S.emit("tensor", lambda E: E.matmul(
                    out=bank(2), lhsT=ones_b[0:1, :], rhs=bv_row[0:1, :],
                    start=False, stop=True).then_inc(sems["pe"], 1))
                S.inc("pe")
                S.wait("vector", "pe")
                vt_dst = vT[:, 528 * tb: 528 * (tb + 1)].rearrange("p (h l) -> p h l", h=16, l=33)[:, :, 0:32]
                pv_src = bank(2).rearrange("p (h l) -> p h l", h=16, l=32)
                S.emit("vector", lambda E, d=vt_dst, s=pv_src: E.tensor_copy(out=d, in_=s).then_inc(sems["dve"], 1))
                S.inc("dve")
                S.wait("tensor", "dve")

            tilebase = tile * 512
            for ft in range(8):
                for ch in range(NCH):
                    S.emit("tensor", lambda E, f=ft, c=ch: E.matmul(
                        out=bank(3), lhsT=Wqk[:, c * 1024 + f * 128: c * 1024 + f * 128 + 128],
                        rhs=xn_f[:, c * 512: (c + 1) * 512], start=(c == 0), stop=(c == NCH - 1)))
                S.tag("tensor", "pe")
                dst = (qp[ft] if ft < 4 else kp[ft - 4])[:, tilebase: tilebase + 512]
                S.wait("scalar", "pe")
                S.emit("scalar", lambda E, d=dst, f=ft: E.activation(
                    out=d, in_=bank(3), func=AF.Identity, bias=bqk_col[:, f: f + 1], scale=1.0).then_inc(sems["act"], 1))
                S.inc("act")
                S.wait("tensor", "act")

        ph1_pe, ph1_dve, ph1_act = S.cnt["pe"], S.cnt["dve"], S.cnt["act"]

        if "d_qp" in dbg_t:
            S.wait("sync", "act", ph1_act)
            for g in range(4):
                S.dma("sync", dbg_t["d_qp"][:, g * T:(g + 1) * T], qp[g][:], "st")
        if "d_kp" in dbg_t:
            S.wait("sync", "act", ph1_act)
            for g in range(4):
                S.dma("sync", dbg_t["d_kp"][:, g * T:(g + 1) * T], kp[g][:], "st")
        if "d_vt" in dbg_t:
            S.wait("sync", "dve", ph1_dve)
            S.dma("sync", dbg_t["d_vt"], vT[:], "st")
        if "d_xn" in dbg_t:
            S.wait("sync", "dve", ph1_dve)
            S.dma("sync", dbg_t["d_xn"], xn_f[:], "st")

        stop_now = stop_now or stop_after == "ph1"
        sqb = xn_f
        for side, grp in ([] if stop_now else [("q", qp), ("k", kp)]):
            for g in range(4):
                S.wait("vector", "act", ph1_act)
                S.emit("vector", lambda E, t_=grp[g]: E.tensor_tensor(
                    out=sqb[:], in0=t_[:], in1=t_[:], op=ALU.mult).then_inc(sems["dve"], 1))
                S.inc("dve")
                S.wait("tensor", "dve")
                col = (0 if side == "q" else 4) + g
                for nt in range(8):
                    S.emit("tensor", lambda E, n=nt: E.matmul(
                        out=PS[0:4, 512 * 4: 512 * 4 + 512], lhsT=eg[:], rhs=sqb[:, 512 * n: 512 * (n + 1)],
                        start=True, stop=True).then_inc(sems["pe"], 1))
                    S.inc("pe")
                    S.wait("vector", "pe")
                    S.emit("vector", lambda E, c=col, n=nt: E.reduce_max(
                        out=gmax2[0:4, 8 * c + n: 8 * c + n + 1],
                        in_=PS[0:4, 512 * 4: 512 * 4 + 512], axis=AX.X).then_inc(sems["dve"], 1))
                    S.inc("dve")
                    S.wait("tensor", "dve")
        if not stop_now:
            S.emit("vector", lambda E: E.memset(smallA[0:2, 10:11], 1.0))
            S.tagwait("vector", "dve")
            S.emit("vector", lambda E: E.reduce_max(out=smallA[0:4, 1:2], in_=gmax2[0:4, 0:32], axis=AX.X))
            S.emit("vector", lambda E: E.reduce_max(out=smallA[0:4, 2:3], in_=gmax2[0:4, 32:64], axis=AX.X).then_inc(sems["dve"], 1))
            S.inc("dve")
            S.wait("tensor", "dve")
            S.emit("tensor", lambda E: E.transpose(
                out=PS[0:2, 512 * 4: 512 * 4 + 4], in_=smallA[0:4, 1:3],
                identity=identf[0:4, 0:4]).then_inc(sems["pe"], 1))
            S.inc("pe")
            S.wait("vector", "pe")
            S.emit("vector", lambda E: E.tensor_copy(out=smallA[0:2, 4:8], in_=PS[0:2, 512 * 4: 512 * 4 + 4]))
            S.tagwait("vector", "dve")
            S.emit("vector", lambda E: E.reduce_max(out=smallA[0:2, 8:9], in_=smallA[0:2, 4:8], axis=AX.X).then_inc(sems["dve"], 1))
            S.inc("dve")
            S.wait("scalar", "dve")
            S.emit("scalar", lambda E: E.activation(out=smallA[0:2, 9:10], in_=smallA[0:2, 8:9], func=AF.Ln).then_inc(sems["act"], 1))
            S.inc("act")
            S.wait("tensor", "act")
            S.emit("tensor", lambda E: E.matmul(
                out=PS[0:1, 512 * 4 + 16: 512 * 4 + 17], lhsT=smallA[0:2, 9:10], rhs=smallA[0:2, 10:11],
                start=True, stop=True).then_inc(sems["pe"], 1))
            S.inc("pe")
            S.wait("vector", "pe")
            S.emit("vector", lambda E: E.tensor_copy(out=rowA[0:1, 24:25], in_=PS[0:1, 512 * 4 + 16: 512 * 4 + 17]).then_inc(sems["dve"], 1))
            S.inc("dve")
            S.wait("scalar", "dve")
            S.emit("scalar", lambda E: E.activation(
                out=rowA[0:1, 26:27], in_=rowA[0:1, 24:25], func=AF.Exp, scale=0.5).then_inc(sems["act"], 1))
            S.inc("act")
            S.wait("vector", "act")
            S.emit("vector", lambda E: E.tensor_scalar(
                out=rowA[0:1, 28:29], in0=rowA[0:1, 26:27], scalar1=80.0, scalar2=None, op0=ALU.min))
            S.tagwait("vector", "dve")
            S.emit("vector", lambda E: E.tensor_scalar(
                out=rowA[0:1, 30:31], in0=rowA[0:1, 28:29], scalar1=-1.0, scalar2=None, op0=ALU.mult).then_inc(sems["dve"], 1))
            S.inc("dve")
            S.wait("tensor", "dve")
            S.emit("tensor", lambda E: E.matmul(
                out=PS[:, 512 * 4 + 8: 512 * 4 + 9], lhsT=ones_f[0:1, :], rhs=rowA[0:1, 30:31],
                start=True, stop=True).then_inc(sems["pe"], 1))
            S.inc("pe")
            S.wait("vector", "pe")
            S.emit("vector", lambda E: E.tensor_copy(out=negc_col[:], in_=PS[:, 512 * 4 + 8: 512 * 4 + 9]).then_inc(sems["dve"], 1))
            S.inc("dve")
            S.wait("scalar", "dve")
            if "d_negc" in dbg_t:
                S.wait("sync", "dve")
                S.dma("sync", dbg_t["d_negc"], negc_col[:], "st")

        stop_now = stop_now or stop_after == "ph15"
        def pv_slice(hp, s):
            idx = hp * 4 + s
            return PS[:, 512 * 4 + 33 * idx: 512 * 4 + 33 * idx + 33]

        lr = ("lrA", "lrB")
        xr_free_dve = {}

        def xr_load(tb):
            if tb - 2 in xr_free_dve:
                S.wait("sync", "dve", xr_free_dve[tb - 2])
            S.dma("sync", xr[tb % 2][:], x_t[128 * tb: 128 * (tb + 1), :], lr[tb % 2])

        if not stop_now:
            S.wait("sync", "pe")
            S.wait("sync", "dve")
            xr_load(0)
            xr_load(1)
        qt_list = [] if stop_now else (range(1) if stop_after == "ph2qt0" else range(NQT))
        def at2af(qtm, rnd):
            aT = aTb[qtm % 2]
            S.wait("tensor", "dve")
            for k in range(8):
                s, lcg = (rnd * 8 + k) // 4, (rnd * 8 + k) % 4
                S.emit("tensor", lambda E, s_=s, l_=lcg, k_=k, aT_=aT: E.transpose(
                    out=PSB[:, 128 * k_: 128 * (k_ + 1)],
                    in_=aT_[:, 512 * s_ + 128 * l_: 512 * s_ + 128 * l_ + 128],
                    identity=identbf[:]))
            S.tag("tensor", "pe")
            S.wait("vector", "pe")
            for k in range(8):
                s, lcg = (rnd * 8 + k) // 4, (rnd * 8 + k) % 4
                S.emit("vector", lambda E, s_=s, l_=lcg, k_=k: E.tensor_copy(
                    out=aF[:, 512 * l_ + 128 * s_: 512 * l_ + 128 * s_ + 128],
                    in_=PSB[:, 128 * k_: 128 * (k_ + 1)]))
            S.emit("vector", lambda E: E.engine_nop().then_inc(sems["dve"], 1))
            S.inc("dve")

        CPJ = PS[:, 2560: 3584]
        RTR = PS[:, 2560: 2568]

        def ph3_block(tb):
            s = tb % 4
            bufi = tb % 2
            S.wait("tensor", "dve")
            for half in range(2):
                for lcg in range(4):
                    S.emit("tensor", lambda E, l_=lcg, s_=s, hf=half: E.matmul(
                        out=CPJ[:, 512 * hf: 512 * hf + 512],
                        lhsT=aF[:, 512 * l_ + 128 * s_: 512 * l_ + 128 * s_ + 128],
                        rhs=Cp[:, l_ * 1024 + hf * 512: l_ * 1024 + hf * 512 + 512],
                        start=(l_ == 0), stop=False))
                S.emit("tensor", lambda E, hf=half: E.matmul(
                    out=CPJ[:, 512 * hf: 512 * hf + 512], lhsT=ones_b[0:1, :],
                    rhs=cpb_row[0:1, hf * 512: hf * 512 + 512], start=False, stop=True))
            S.tag("tensor", "pe")
            S.wait("vector", "pe")
            S.wait("vector", lr[bufi], S.cnt[lr[bufi]])
            S.wait("vector", "sx2", S.cnt["sx2"])
            S.emit("vector", lambda E, b=bufi: E.tensor_tensor(
                out=x2_t[:], in0=CPJ[:], in1=xr[b][:], op=ALU.add).then_inc(sems["dve"], 1))
            S.inc("dve")
            xr_free_dve[tb] = S.cnt["dve"]
            x2_ready = S.cnt["dve"]
            if tb + 2 < NTB:
                xr_load(tb + 2)
            S.wait("sync", "dve")
            S.dma("sync", x2s[128 * tb: 128 * (tb + 1), :], x2_t[:], "sx2")
            S.emit("vector", lambda E: E.reduce_sum(out=s12[:, 0:1], in_=x2_t[:], axis=AX.X))
            S.tagwait("vector", "dve")
            S.wait("scalar", "dve", x2_ready)
            S.emit("scalar", lambda E: E.activation(
                out=probsT[0][:, 1024:2048], in_=x2_t[:], func=AF.Square, accum_out=s12[:, 4:5]).then_inc(sems["act"], 1))
            S.inc("act")
            S.wait("vector", "act")
            S.emit("vector", lambda E: E.tensor_scalar(
                out=mcol[:], in0=s12[:, 0:1], scalar1=1.0 / D, scalar2=None, op0=ALU.mult))
            S.emit("vector", lambda E: E.tensor_scalar(
                out=vcol[:], in0=s12[:, 4:5], scalar1=1.0 / D, scalar2=1e-5, op0=ALU.mult, op1=ALU.add))
            S.tagwait("vector", "dve")
            S.emit("vector", lambda E: E.tensor_tensor(out=msq[:], in0=mcol[:], in1=mcol[:], op=ALU.mult))
            S.emit("vector", lambda E: E.tensor_scalar(
                out=tmp_f[:], in0=x2_t[:], scalar1=mcol[:, 0:1], scalar2=None, op0=ALU.subtract))
            S.tagwait("vector", "dve")
            S.emit("vector", lambda E: E.tensor_tensor(
                out=vcol[:], in0=vcol[:], in1=msq[:], op=ALU.subtract).then_inc(sems["dve"], 1))
            S.inc("dve")
            S.wait("scalar", "dve")
            S.emit("scalar", lambda E: E.activation(out=lncol[:], in_=vcol[:], func=AF.Ln))
            S.emit("scalar", lambda E: E.activation(
                out=rstd[:], in_=lncol[:], func=AF.Exp, scale=-0.5).then_inc(sems["act"], 1))
            S.inc("act")
            S.wait("vector", "act")
            S.emit("vector", lambda E: E.tensor_scalar(
                out=xn2_t[:], in0=tmp_f[:], scalar1=rstd[:, 0:1], scalar2=None, op0=ALU.mult).then_inc(sems["dve"], 1))
            S.inc("dve")
            S.wait("tensor", "dve")
            for ch in range(NCH):
                S.emit("tensor", lambda E, c=ch: E.transpose(
                    out=PSB[:, 128 * c: 128 * (c + 1)], in_=xn2_t[:, 128 * c: 128 * (c + 1)],
                    identity=identbf[:]))
            S.tag("tensor", "pe")
            S.wait("vector", "pe")
            S.wait("vector", "sxn", S.cnt["sxn"])
            S.emit("vector", lambda E: E.tensor_copy(out=xn2_f[:], in_=PSB[:]).then_inc(sems["dve"], 1))
            S.inc("dve")
            S.wait("sync", "dve")
            S.dma("sync", xn2s[tb], xn2_f[:], "sxn")
            S.wait("tensor", "dve")
            S.wait("tensor", "gio", 32)
            for ch in range(NCH):
                S.emit("tensor", lambda E, c=ch: E.matmul(
                    out=RTR, lhsT=xn2_f[:, 128 * c: 128 * (c + 1)],
                    rhs=rW[:, 8 * c: 8 * (c + 1)], start=(c == 0), stop=False))
            S.emit("tensor", lambda E: E.matmul(
                out=RTR, lhsT=ones_b[0:1, :], rhs=rb_row[0:1, :],
                start=False, stop=True).then_inc(sems["pe"], 1))
            S.inc("pe")
            S.wait("scalar", "pe")
            S.emit("scalar", lambda E: E.activation(
                out=pr8[:], in_=RTR, func=AF.Exp, scale=-1.0).then_inc(sems["act"], 1))
            S.inc("act")
            S.wait("vector", "act")
            S.emit("vector", lambda E: E.tensor_scalar(
                out=top_a[:], in0=pr8[:], scalar1=1.0, scalar2=None, op0=ALU.add))
            S.tagwait("vector", "dve")
            S.emit("vector", lambda E: E.reciprocal(out=pr8b[:], in_=top_a[:]))
            S.tagwait("vector", "dve")
            S.emit("vector", lambda E: E.reduce_max(out=mcol[:], in_=pr8b[:], axis=AX.X))
            S.tagwait("vector", "dve")
            S.emit("vector", lambda E: E.tensor_scalar(
                out=top_a[:], in0=pr8b[:], scalar1=mcol[:, 0:1], scalar2=None, op0=ALU.is_ge))
            S.tagwait("vector", "dve")
            S.emit("vector", lambda E: E.scalar_tensor_tensor(
                out=top_b[:], in0=top_a[:], scalar=-2.0, in1=pr8b[:], op0=ALU.mult, op1=ALU.add))
            S.tagwait("vector", "dve")
            S.emit("vector", lambda E: E.reduce_max(out=msq[:], in_=top_b[:], axis=AX.X))
            S.tagwait("vector", "dve")
            S.emit("vector", lambda E: E.tensor_scalar(
                out=top_a[:], in0=pr8b[:], scalar1=msq[:, 0:1], scalar2=None, op0=ALU.is_ge))
            S.tagwait("vector", "dve")
            S.emit("vector", lambda E: E.tensor_tensor(out=top_b[:], in0=pr8b[:], in1=top_a[:], op=ALU.mult))
            S.tagwait("vector", "dve")
            S.emit("vector", lambda E: E.reduce_sum(out=vcol[:], in_=top_b[:], axis=AX.X))
            S.tagwait("vector", "dve")
            S.emit("vector", lambda E: E.reciprocal(out=lncol[:], in_=vcol[:]))
            S.tagwait("vector", "dve")
            S.emit("vector", lambda E, t_=tb: E.tensor_scalar(
                out=w_tok[:, 8 * t_: 8 * (t_ + 1)], in0=top_b[:],
                scalar1=lncol[:, 0:1], scalar2=None, op0=ALU.mult).then_inc(sems["dve"], 1))
            S.inc("dve")

        exp_inc = {0: 0, 1: 0}
        evac_inc = {0: 0, 1: 0}
        pv_gate = {0: 0, 1: 0}
        qt_seq = list(qt_list)
        tail = [qt_seq[-1] + 1] if qt_seq else []
        for qtx in qt_seq + tail:
            run_attn = qtx in qt_seq
            nkc = 4 * qtx + 4
            aT = aTb[qtx % 2]
            for g2 in (range(8) if run_attn else []):
                if qtx >= 1:
                    if g2 == 0:
                        at2af(qtx - 1, 0)
                    elif g2 == 1:
                        at2af(qtx - 1, 1)
                    elif g2 in (2, 3, 4, 5):
                        ph3_block(4 * (qtx - 1) + (g2 - 2))
                gq = g2 // 2
                po = 64 * (g2 % 2)

                def emit_pack(kc):
                    jdiag = kc - 4 * qtx
                    p = kc % 2
                    base = 1024 * p
                    S.wait("tensor", "dve", evac_inc[p])
                    for hp in range(2):
                        trim = 128 * jdiag if jdiag > 0 else 0
                        row = po + 32 * hp
                        S.emit("tensor", lambda E, r=row, h_=hp, k_=kc, tr=trim,
                               q0=qtx * 512, b0=base, gq_=gq: E.matmul(
                            out=PS[:, b0 + 512 * h_ + tr: b0 + 512 * h_ + 512],
                            lhsT=kp[gq_][r: r + 32, 128 * k_: 128 * k_ + 128],
                            rhs=qp[gq_][r: r + 32, q0 + tr: q0 + 512],
                            start=True, stop=True, tile_position=(r, 0)))
                    S.tag("tensor", "pe")
                    pe_pack = S.cnt["pe"]
                    S.wait("vector", "pe", pe_pack)
                    S.wait("vector", "act", exp_inc[p])
                    if jdiag >= 0:
                        for hp in range(2):
                            off = base + 512 * hp + 128 * jdiag
                            S.emit("vector", lambda E, o=off: E.tensor_tensor(
                                out=PS[:, o: o + 128], in0=PS[:, o: o + 128], in1=trimask[:], op=ALU.add))
                    S.emit("vector", lambda E, b=p, b0=base: E.tensor_copy(
                        out=scoresb[b], in_=PS[:, b0: b0 + 1024]).then_inc(sems["dve"], 1))
                    evac_inc[p] = S.inc("dve")
                    S.wait("scalar", "dve", evac_inc[p])
                    if pv_gate[p]:
                        S.wait("scalar", "pe", pv_gate[p])
                    S.emit("scalar", lambda E, b=p: E.activation(
                        out=probsT[b][:, 0:1024], in_=scoresb[b], func=AF.Exp,
                        bias=negc_col[:, 0:1], scale=1.0).then_inc(sems["act"], 1))
                    exp_inc[p] = S.inc("act")

                def emit_pv(kc):
                    p = kc % 2
                    S.wait("tensor", "act", exp_inc[p])
                    pv_gate[p] = None
                    for hp in range(2):
                        h = 2 * g2 + hp
                        for s in range(4):
                            if kc > 4 * qtx + s:
                                continue
                            idx = hp * 4 + s
                            S.emit("tensor", lambda E, b=p, h_=hp, hh=h, s_=s, k_=kc,
                                   st_=(kc == 0 and idx == 0), sp=(kc == 4 * qtx + s): E.matmul(
                                out=pv_slice(h_, s_),
                                lhsT=probsT[b][:, 512 * h_ + 128 * s_: 512 * h_ + 128 * s_ + 128],
                                rhs=vT[:, 528 * k_ + 33 * hh: 528 * k_ + 33 * hh + 33],
                                start=st_, stop=sp, skip_group_check=True))
                    S.tag("tensor", "pe")
                    pv_gate[p] = S.cnt["pe"]

                for kc in range(nkc):
                    emit_pack(kc)
                    if kc >= 1:
                        emit_pv(kc - 1)
                emit_pv(nkc - 1)
                S.wait("vector", "pe")
                for hp in range(2):
                    for s in range(4):
                        S.emit("vector", lambda E, h_=hp, s_=s: E.reciprocal(
                            out=rlcol[:, h_ * 4 + s_: h_ * 4 + s_ + 1], in_=pv_slice(h_, s_)[:, 32:33]))
                S.tagwait("vector", "dve")
                for hp in range(2):
                    h = 2 * g2 + hp
                    for s in range(4):
                        S.emit("vector", lambda E, h_=hp, hh=h, s_=s, aT_=aT: E.tensor_scalar(
                            out=aT_[:, 512 * s_ + 32 * hh: 512 * s_ + 32 * hh + 32],
                            in0=pv_slice(h_, s_)[:, 0:32],
                            scalar1=rlcol[:, h_ * 4 + s_: h_ * 4 + s_ + 1], scalar2=None, op0=ALU.mult))
                S.emit("vector", lambda E: E.engine_nop().then_inc(sems["dve"], 1))
                S.inc("dve")
                S.wait("tensor", "dve")

            if not run_attn and qtx >= 1:
                if "d_at" in dbg_t and qtx - 1 == 0:
                    S.wait("sync", "dve")
                    for s in range(4):
                        S.dma("sync", dbg_t["d_at"][s], aTb[(qtx - 1) % 2][:, 512 * s: 512 * (s + 1)], "st")
                at2af(qtx - 1, 0)
                at2af(qtx - 1, 1)
                if "d_af" in dbg_t and qtx - 1 == 0:
                    S.wait("sync", "dve")
                    S.dma("sync", dbg_t["d_af"], aF[:], "st")
                for s in range(4):
                    ph3_block(4 * (qtx - 1) + s)

        ph3_pe, ph3_dve, ph3_act = S.cnt["pe"], S.cnt["dve"], S.cnt["act"]
        ph3_sx2, ph3_sxn = S.cnt["sx2"], S.cnt["sxn"]

        if "d_x2" in dbg_t:
            S.wait("sync", "sx2", ph3_sx2)
            S.dma("sync", dbg_t["d_x2"], x2s, "st")
        if "d_w" in dbg_t:
            S.wait("sync", "dve", ph3_dve)
            S.dma("sync", dbg_t["d_w"], w_tok[:], "st")
        if "d_xn2" in dbg_t:
            S.wait("sync", "sxn", ph3_sxn)
            for tb in range(NTB):
                S.dma("sync", dbg_t["d_xn2"][tb], xn2s[tb], "st")

        st12.close()
        stop_now = stop_now or stop_after in ("ph2qt0", "ph3")

        st4 = ExitStack()
        xn2q = SB(st4, "xn2q", [128, 8 * 1024], BF16)
        acc = SB(st4, "acc", [128, 8 * 1024], F32)
        wG = [SB(st4, "gbufA", [128, 8 * 1024], BF16), SB(st4, "gbufB", [128, 8 * 1024], BF16)]
        wU = [SB(st4, "ubufA", [128, 8 * 1024], BF16), SB(st4, "ubufB", [128, 8 * 1024], BF16)]
        wD = [SB(st4, "dbufA", [128, 8 * 1024], BF16), SB(st4, "dbufB", [128, 8 * 1024], BF16)]
        gsb = SB(st4, "gsb", [128, 8 * 512], BF16)
        hsb = SB(st4, "hsb", [128, 8 * 512], BF16)

        def gate_src(e, ih):
            return (sgT if e < 0 else gw[e])[:, ih * 1024: (ih + 1) * 1024]

        def up_src(e, ih):
            return (suT if e < 0 else uw[e])[:, ih * 1024: (ih + 1) * 1024]

        def down_src(e, ih):
            return (sdT if e < 0 else dw[e])[ih * 1024: (ih + 1) * 1024, :]

        wsem = {0: ("wgA", "wuA", "wdA"), 1: ("wgB", "wuB", "wdB")}
        pe_free_at = {0: 0, 1: 0}
        silu_inc = [0, 0, 0, 0]
        htt_inc = [0, 0, 0, 0]
        pair_dve = [0, 0]
        dbank_free = {4: 0, 5: 0, 6: 0}
        dctr = [0]
        EXPERTS = [-1] + list(range(E))
        halves = [(e, ih) for e in EXPERTS for ih in range(2)]

        S.wait("gpsimd", "pe", ph3_pe)
        S.wait("gpsimd", "dve", ph3_dve)
        S.wait("gpsimd", "act", ph3_act)
        S.wait("sync", "pe", ph3_pe)
        S.wait("sync", "dve", ph3_dve)
        S.wait("sync", "act", ph3_act)

        def load_half(e, ih, slot):
            sg_, su_, sd_ = wsem[slot]
            for ch in range(NCH):
                S.dma("gpsimd", wG[slot][:, ch * 1024: (ch + 1) * 1024],
                      gate_src(e, ih)[128 * ch: 128 * (ch + 1), :], sg_)
            for ch in range(NCH):
                S.dma("gpsimd", wU[slot][:, ch * 1024: (ch + 1) * 1024],
                      up_src(e, ih)[128 * ch: 128 * (ch + 1), :], su_)
            for ch in range(NCH):
                S.dma("gpsimd", wD[slot][:, ch * 1024: (ch + 1) * 1024],
                      down_src(e, ih)[128 * ch: 128 * (ch + 1), :], sd_)

        for Qi in ([] if stop_now else range(NQ)):
            S.wait("sync", "sx2", ph3_sx2)
            S.wait("sync", "sxn", ph3_sxn)
            if Qi > 0:
                S.wait("sync", "pe", S.cnt["pe"])
                S.wait("sync", "dve", S.cnt["dve"])
                S.wait("sync", "st", S.cnt["st"])
            for k in range(8):
                tb = Qi * 8 + k
                src = xn2s[tb].rearrange("p (c t) -> p c t", c=8)
                dstq = xn2q[:].rearrange("p (c q t) -> p c q t", c=8, q=8, t=128)[:, :, k, :]
                S.dma("sync", dstq, src, "lq")
                S.dma("sync", acc[:, k * 1024: (k + 1) * 1024], x2s[128 * tb: 128 * (tb + 1), :], "lq")

            for hi, (e, ih) in enumerate(halves):
                slot = hi % 2
                sg_, su_, sd_ = wsem[slot]
                S.wait("gpsimd", "pe", pe_free_at[slot])
                load_half(e, ih, slot)
                S.wait("tensor", "lq", S.cnt["lq"])
                for tt in range(2):
                    for pr in range(4):
                        pb0 = 0 if pr % 2 == 0 else 2
                        S.wait("tensor", sg_, S.cnt[sg_])
                        S.wait("tensor", "dve", pair_dve[pr % 2])
                        for it in (2 * pr, 2 * pr + 1):
                            bnk = pb0 + (it % 2)
                            for ch in range(NCH):
                                S.emit("tensor", lambda E, b=bnk, i_=it, c=ch, t_=tt, sl=slot: E.matmul(
                                    out=bank(b), lhsT=wG[sl][:, c * 1024 + 128 * i_: c * 1024 + 128 * i_ + 128],
                                    rhs=xn2q[:, c * 1024 + 512 * t_: c * 1024 + 512 * t_ + 512],
                                    start=(c == 0), stop=(c == NCH - 1)))
                        S.tag("tensor", "pe")
                        S.wait("scalar", "pe")
                        S.wait("scalar", "dve", htt_inc[pr])
                        S.emit("scalar", lambda E, b=pb0, p_=pr: E.activation(
                            out=gsb[:, 1024 * p_: 1024 * (p_ + 1)], in_=PS[:, 512 * b: 512 * b + 1024],
                            func=AF.Silu).then_inc(sems["act"], 1))
                        silu_inc[pr] = S.inc("act")
                    for pr in range(4):
                        pb0 = 0 if pr % 2 == 0 else 2
                        S.wait("tensor", su_, S.cnt[su_])
                        S.wait("tensor", "act", silu_inc[pr])
                        for it in (2 * pr, 2 * pr + 1):
                            bnk = pb0 + (it % 2)
                            for ch in range(NCH):
                                S.emit("tensor", lambda E, b=bnk, i_=it, c=ch, t_=tt, sl=slot: E.matmul(
                                    out=bank(b), lhsT=wU[sl][:, c * 1024 + 128 * i_: c * 1024 + 128 * i_ + 128],
                                    rhs=xn2q[:, c * 1024 + 512 * t_: c * 1024 + 512 * t_ + 512],
                                    start=(c == 0), stop=(c == NCH - 1)))
                        S.tag("tensor", "pe")
                        S.wait("vector", "pe")
                        S.wait("vector", "act", silu_inc[pr])
                        S.emit("vector", lambda E, b=pb0, p_=pr: E.tensor_tensor(
                            out=hsb[:, 1024 * p_: 1024 * (p_ + 1)], in0=gsb[:, 1024 * p_: 1024 * (p_ + 1)],
                            in1=PS[:, 512 * b: 512 * b + 1024], op=ALU.mult).then_inc(sems["dve"], 1))
                        htt_inc[pr] = S.inc("dve")
                        pair_dve[pr % 2] = htt_inc[pr]
                    S.wait("tensor", sd_, S.cnt[sd_])
                    S.wait("tensor", "dve", htt_inc[3])
                    for j in range(4):
                        tbq = tt * 4 + j
                        tbg = Qi * 8 + tbq
                        bks = []
                        for dh in range(2):
                            b = 4 + (dctr[0] % 3)
                            dctr[0] += 1
                            bks.append(b)
                            S.wait("tensor", "dve", dbank_free[b])
                            for ic in range(NCH):
                                co = (ic // 2) * 1024 + (ic % 2) * 512 + 128 * j
                                S.emit("tensor", lambda E, b_=b, c=ic, co_=co, d_=dh, sl=slot: E.matmul(
                                    out=bank(b_), lhsT=hsb[:, co_: co_ + 128],
                                    rhs=wD[sl][:, c * 1024 + d_ * 512: c * 1024 + d_ * 512 + 512],
                                    start=(c == 0), stop=(c == NCH - 1)))
                        S.tag("tensor", "pe")
                        S.wait("vector", "pe")
                        for dh in range(2):
                            b = bks[dh]
                            aslice = acc[:, 1024 * tbq + 512 * dh: 1024 * tbq + 512 * dh + 512]
                            if e < 0:
                                S.emit("vector", lambda E, b_=b, a_=aslice: E.tensor_tensor(
                                    out=a_, in0=bank(b_), in1=a_, op=ALU.add).then_inc(sems["dve"], 1))
                            else:
                                S.emit("vector", lambda E, b_=b, a_=aslice, e_=e, tg=tbg: E.scalar_tensor_tensor(
                                    out=a_, in0=bank(b_), scalar=w_tok[:, 8 * tg + e_: 8 * tg + e_ + 1],
                                    in1=a_, op0=ALU.mult, op1=ALU.add).then_inc(sems["dve"], 1))
                            dbank_free[b] = S.inc("dve")
                pe_free_at[slot] = S.cnt["pe"]

            S.wait("sync", "dve", S.cnt["dve"])
            for k in range(8):
                tb = Qi * 8 + k
                S.dma("sync", out[128 * tb: 128 * (tb + 1), :], acc[:, k * 1024: (k + 1) * 1024], "st")

        st4.close()

        S.wait("sync", "st", S.cnt["st"])
        S.wait("gpsimd", "st", S.cnt["st"])

        with nc.Block() as block:
            @block.sync
            def _(e):
                for f in S.q["sync"]:
                    f(e)

            @block.gpsimd
            def _(e):
                for f in S.q["gpsimd"]:
                    f(e)

            @block.vector
            def _(e):
                for f in S.q["vector"]:
                    f(e)

            @block.scalar
            def _(e):
                for f in S.q["scalar"]:
                    f(e)

            @block.tensor
            def _(e):
                for f in S.q["tensor"]:
                    f(e)

    return nc


def _host_inputs(inputs):
    f32 = np.float32
    x = np.asarray(inputs["L0_attn_hidden_states"], f32)
    ca = np.asarray(inputs["L0_attn_c_attn_w"], f32)
    cab = np.asarray(inputs["L0_attn_c_attn_b"], f32)
    cpw_ = np.asarray(inputs["L0_attn_c_proj_w"], f32)
    cpb_ = np.asarray(inputs["L0_attn_c_proj_b"], f32)
    wk_c = np.asarray(inputs["L0_attn_wk_c"], f32)
    wk_e = np.asarray(inputs["L0_attn_wk_e"], f32)
    wv_c = np.asarray(inputs["L0_attn_wv_c"], f32)
    wv_e = np.asarray(inputs["L0_attn_wv_e"], f32)
    sg = np.asarray(inputs["L0_ffn_shared_gate"], f32)
    su = np.asarray(inputs["L0_ffn_shared_up"], f32)
    sd = np.asarray(inputs["L0_ffn_shared_down"], f32)
    gwv = np.asarray(inputs["L0_ffn_gate_w"], f32)
    uwv = np.asarray(inputs["L0_ffn_up_w"], f32)
    dwv = np.asarray(inputs["L0_ffn_down_w"], f32)
    rw = np.asarray(inputs["L0_ffn_router_w"], f32)
    rbv = np.asarray(inputs["L0_ffn_routing_bias"], f32)

    shared = {
        "caT": np.ascontiguousarray(ca.T),
        "b_mat": np.ascontiguousarray(cab.reshape(3, 16, 64).transpose(2, 0, 1).reshape(64, 48)),
        "wkeTh": np.ascontiguousarray(wk_e.T),
        "wkch": np.ascontiguousarray(wk_c),
        "wvch": np.ascontiguousarray(wv_c),
        "wveTh": np.ascontiguousarray(wv_e.T),
        "cpw": np.ascontiguousarray(cpw_),
        "cpb": np.ascontiguousarray(cpb_.reshape(1, D)),
        "rw_pk": np.ascontiguousarray(rw.T.reshape(8, 128, 8).transpose(1, 0, 2).reshape(128, 64)),
        "rb": np.ascontiguousarray(rbv.reshape(1, E)),
        "sgT": np.ascontiguousarray(sg.T),
        "suT": np.ascontiguousarray(su.T),
        "sdT": np.ascontiguousarray(sd.T),
        "gw": np.ascontiguousarray(gwv),
        "uw": np.ascontiguousarray(uwv),
        "dw": np.ascontiguousarray(dwv),
        "identbf": np.eye(128, dtype=BF16NP),
        "identf": np.eye(128, dtype=f32),
        "trimask": np.where(np.arange(128)[:, None] > np.arange(128)[None, :],
                            f32(-1e30), f32(0.0)).astype(f32),
        "eg": (np.arange(128)[:, None] // 32 == np.arange(4)[None, :]).astype(BF16NP),
    }
    in_maps = []
    for c in range(8):
        m = dict(shared)
        m["x_t"] = np.ascontiguousarray(x[c])
        in_maps.append(m)
    return in_maps


def kernel(**inputs):
    in_maps = _host_inputs(inputs)
    nc = build_program()
    res = run_bass_kernel_spmd(nc, in_maps, core_ids=list(range(8)))
    return np.stack([r["out"] for r in res.results], axis=0)


# version 06
# speedup vs baseline: 1.3136x; 1.3136x over previous
import numpy as np
import ml_dtypes
from contextlib import ExitStack
import concourse.bass as bass
import concourse.mybir as mybir
from concourse.bass_utils import run_bass_kernel_spmd

F32 = mybir.dt.float32
BF16 = mybir.dt.bfloat16
AF = mybir.ActivationFunctionType
ALU = mybir.AluOpType
AX = mybir.AxisListType
BF16NP = ml_dtypes.bfloat16

T = 4096
D = 1024
NH = 16
R = 32
I = 2048
E = 8
NTB = T // 128
NQT = T // 512
NCH = D // 128
NQ = 4
ENGS = ("sync", "gpsimd", "vector", "scalar", "tensor")


class Sched:
    def __init__(self, sems):
        self.sems = sems
        self.q = {e: [] for e in ENGS}
        self.cnt = {k: 0 for k in sems}
        self.lastwait = {}

    def emit(self, eng, fn):
        self.q[eng].append(fn)

    def wait(self, eng, sem, val=None):
        if val is None:
            val = self.cnt[sem]
        if val is None or val <= 0:
            return
        key = (eng, sem)
        if self.lastwait.get(key, -1) >= val:
            return
        self.lastwait[key] = val
        h = self.sems[sem]
        self.q[eng].append(lambda E, h=h, v=val: E.wait_ge(h, v))

    def inc(self, sem, n=1):
        self.cnt[sem] += n
        return self.cnt[sem]

    def tag(self, eng, sem, n=1):
        fn = self.q[eng].pop()
        h = self.sems[sem]
        self.q[eng].append(lambda E, f=fn, h=h, n=n: f(E).then_inc(h, n))
        return self.inc(sem, n)

    def tagwait(self, eng, sem):
        v = self.tag(eng, sem)
        self.wait(eng, sem, v)
        return v

    def dma(self, eng, dst, src, sem):
        h = self.sems[sem]
        self.q[eng].append(lambda E, d=dst, s=src, h=h: E.dma_start(out=d, in_=s).then_inc(h, 16))
        return self.inc(sem, 16)


def build_program(dbg=(), stop_after=None):
    nc = bass.Bass("TRN2", target_bir_lowering=False)

    x_t = nc.dram_tensor("x_t", [T, D], F32, kind="ExternalInput").ap()
    caT = nc.dram_tensor("caT", [3 * D, D], F32, kind="ExternalInput").ap()
    b_mat = nc.dram_tensor("b_mat", [64, 48], F32, kind="ExternalInput").ap()
    wkeTh = nc.dram_tensor("wkeTh", [64, 32], F32, kind="ExternalInput").ap()
    wkch = nc.dram_tensor("wkch", [64, 32], F32, kind="ExternalInput").ap()
    wvch = nc.dram_tensor("wvch", [64, 32], F32, kind="ExternalInput").ap()
    wveTh = nc.dram_tensor("wveTh", [64, 32], F32, kind="ExternalInput").ap()
    cpw = nc.dram_tensor("cpw", [D, D], F32, kind="ExternalInput").ap()
    cpb = nc.dram_tensor("cpb", [1, D], F32, kind="ExternalInput").ap()
    rw_pk = nc.dram_tensor("rw_pk", [128, 64], F32, kind="ExternalInput").ap()
    rb = nc.dram_tensor("rb", [1, E], F32, kind="ExternalInput").ap()
    sgT = nc.dram_tensor("sgT", [D, I], F32, kind="ExternalInput").ap()
    suT = nc.dram_tensor("suT", [D, I], F32, kind="ExternalInput").ap()
    sdT = nc.dram_tensor("sdT", [I, D], F32, kind="ExternalInput").ap()
    gw = nc.dram_tensor("gw", [E, D, I], F32, kind="ExternalInput").ap()
    uw = nc.dram_tensor("uw", [E, D, I], F32, kind="ExternalInput").ap()
    dw = nc.dram_tensor("dw", [E, I, D], F32, kind="ExternalInput").ap()
    identbf_h = nc.dram_tensor("identbf", [128, 128], BF16, kind="ExternalInput").ap()
    identf_h = nc.dram_tensor("identf", [128, 128], F32, kind="ExternalInput").ap()
    trimask_h = nc.dram_tensor("trimask", [128, 128], F32, kind="ExternalInput").ap()
    eg_h = nc.dram_tensor("eg", [128, 4], BF16, kind="ExternalInput").ap()

    out = nc.dram_tensor("out", [T, D], F32, kind="ExternalOutput").ap()
    x2s = nc.dram_tensor("x2s", [T, D], F32, kind="Internal").ap()
    xn2s = nc.dram_tensor("xn2s", [NTB, 128, D], BF16, kind="Internal").ap()

    dbg_t = {}
    for name, shape, dt in (
        ("d_qp", [128, 4 * T], BF16), ("d_kp", [128, 4 * T], BF16), ("d_vt", [128, 32 * 528], BF16),
        ("d_xn", [128, 8 * 512], BF16), ("d_x2", [T, D], F32), ("d_xn2", [NTB, 128, D], BF16),
        ("d_w", [128, NTB * 8], F32), ("d_at", [4, 128, 512], BF16), ("d_negc", [128, 1], F32),
        ("d_xnt", [128, D], BF16), ("d_cols", [128, 8], F32), ("d_af", [128, 2048], BF16),
    ):
        if name in dbg:
            dbg_t[name] = nc.dram_tensor(name, shape, dt, kind="ExternalOutput").ap()

    with ExitStack() as stack:
        SEMNAMES = ("lw0", "gio", "lfA", "lfB", "lcA", "lcB", "lxA", "lxB", "lrA", "lrB",
                    "sx2", "sxn", "lq", "wgA", "wgB", "wuA", "wuB", "wdA", "wdB",
                    "st", "pe", "dve", "act")
        sems = {n: stack.enter_context(nc.semaphore(n)) for n in SEMNAMES}
        S = Sched(sems)

        def SB(st_, name, shape, dt):
            return st_.enter_context(nc.sbuf_tensor(name, shape, dt))

        identbf = SB(stack, "identbf_s", [128, 128], BF16)
        identf = SB(stack, "identf_s", [128, 128], F32)
        trimask = SB(stack, "trimask_s", [128, 128], F32)
        eg = SB(stack, "eg_s", [128, 4], BF16)
        ones_b = SB(stack, "ones_b", [1, 128], BF16)
        ones_f = SB(stack, "ones_f", [1, 128], F32)
        w_tok = SB(stack, "w_tok", [128, NTB * 8], F32)
        negc_col = SB(stack, "negc_col", [128, 1], F32)
        smallA = SB(stack, "smallA", [128, 16], F32)
        rowA = SB(stack, "rowA", [1, 64], F32)
        rW = SB(stack, "rW", [128, 64], BF16)
        rb_row = SB(stack, "rb_row", [1, E], BF16)
        jcol = SB(stack, "jcol", [128, 1], F32)

        PS = stack.enter_context(nc.psum_tensor("PS", [128, 3584], F32))
        PSB = stack.enter_context(nc.psum_tensor("PSB", [128, 1024], BF16))

        def bank(i):
            return PS[:, 512 * i: 512 * (i + 1)]

        st12 = ExitStack()
        qp = [SB(st12, f"qp{g}", [128, T], BF16) for g in range(4)]
        kp = [SB(st12, f"kp{g}", [128, T], BF16) for g in range(4)]
        vT = SB(st12, "vT", [128, 32 * 528], BF16)
        Wqk = SB(st12, "Wqk", [128, 8 * 1024], BF16)
        Wv = SB(st12, "Wv", [128, 8 * 512], BF16)
        Cp = SB(st12, "Cp", [128, 4 * 1024], BF16)
        bqk_col = SB(st12, "bqk_col", [128, 8], F32)
        bv_rowf = SB(st12, "bv_rowf", [1, 512], F32)
        bv_row = SB(st12, "bv_row", [1, 512], BF16)
        cpb_rowf = SB(st12, "cpb_rowf", [1, D], F32)
        cpb_row = SB(st12, "cpb_row", [1, D], BF16)
        wk_s = SB(st12, "wk_s", [64, 32 * 4], F32)
        wkeT_s = SB(st12, "wkeT_s", [64, 32], F32)
        bmat_s = SB(st12, "bmat_s", [64, 48], F32)
        xt = [SB(st12, f"xt{i}", [128, D], F32) for i in range(2)]
        xr = [SB(st12, f"xr{i}", [128, D], F32) for i in range(2)]
        wfold = [xr[0][0:64, :], xr[1][0:64, :]]
        cpslc = [xt[0][0:64, :], xt[1][0:64, :]]
        tmp_f = SB(st12, "tmp_f", [128, D], F32)
        s12 = SB(st12, "s12", [128, 8], F32)
        mcol = SB(st12, "mcol", [128, 1], F32)
        msq = SB(st12, "msq", [128, 1], F32)
        vcol = SB(st12, "vcol", [128, 1], F32)
        lncol = SB(st12, "lncol", [128, 1], F32)
        rstd = SB(st12, "rstd", [128, 1], F32)
        xn_t = SB(st12, "xn_t", [128, D], BF16)
        xn_f = SB(st12, "xn_f", [128, 8 * 512], BF16)
        gmax2 = SB(st12, "gmax2", [4, 64], F32)
        probsT = [SB(st12, f"probsT{i}", [128, 2048], BF16) for i in range(2)]
        aTb = [SB(st12, "aT0", [128, 4 * 512], BF16), SB(st12, "aT1", [128, 4 * 512], BF16)]
        aF = SB(st12, "aF", [128, 2048], BF16)
        rlcol = SB(st12, "rlcol", [128, 16], F32)
        scoresb = [xt[0][:], xt[1][:]]
        x2_t = SB(st12, "x2_t", [128, D], F32)
        xn2_t = SB(st12, "xn2_t", [128, D], BF16)
        xn2_f = SB(st12, "xn2_f", [128, D], BF16)
        pr8 = SB(st12, "pr8", [128, E], F32)
        pr8b = SB(st12, "pr8b", [128, E], F32)
        top_a = SB(st12, "top_a", [128, E], F32)
        top_b = SB(st12, "top_b", [128, E], F32)

        for dst, src in ((identbf[:], identbf_h), (identf[:], identf_h), (trimask[:], trimask_h),
                         (eg[:], eg_h), (wk_s[:, 0:32], wkeTh), (wk_s[:, 32:64], wkch),
                         (wk_s[:, 64:96], wvch), (wk_s[:, 96:128], wveTh), (bmat_s[:], b_mat),
                         (cpb_rowf[:], cpb)):
            S.dma("sync", dst, src, "lw0")
        S.dma("gpsimd", rW[:], rw_pk, "gio")
        S.dma("gpsimd", rb_row[:], rb, "gio")

        S.emit("vector", lambda E: E.memset(ones_b[:], 1.0))
        S.emit("vector", lambda E: E.memset(ones_f[:], 1.0))
        S.emit("vector", lambda E: E.memset(vT[:], 1.0))
        S.emit("vector", lambda E: E.memset(s12[:], 0.0))
        S.emit("vector", lambda E: E.memset(gmax2[:], 0.0).then_inc(sems["dve"], 1))
        S.inc("dve")

        S.wait("scalar", "lw0")
        S.emit("scalar", lambda E: E.mul(out=wkeT_s[:], in_=wk_s[:, 0:32], mul=0.125))
        S.emit("scalar", lambda E: E.copy(out=cpb_row[:], in_=cpb_rowf[:]).then_inc(sems["act"], 1))
        S.inc("act")
        act_ph0 = S.cnt["act"]

        fold_jobs = [(p, h) for p in range(3) for h in range(NH)]
        if stop_after == "ph0":
            fold_jobs = []
        elif stop_after == "fold1":
            fold_jobs = fold_jobs[:1]
        elif stop_after == "fold3":
            fold_jobs = fold_jobs[:3]
        elif stop_after == "foldq":
            fold_jobs = fold_jobs[:16]
        elif stop_after == "foldqkv":
            pass
        elif stop_after == "foldcp":
            fold_jobs = []
        lf = ("lfA", "lfB")

        def fold_load(j):
            p, h = fold_jobs[j]
            S.dma("sync", wfold[j % 2], caT[p * D + 64 * h: p * D + 64 * h + 64, :], lf[j % 2])

        if fold_jobs:
            fold_load(0)
        if len(fold_jobs) > 1:
            fold_load(1)
        S.wait("tensor", "lw0")
        S.wait("tensor", "act", act_ph0)
        rhs_map = {0: wkeT_s[:], 1: wk_s[:, 32:64], 2: wk_s[:, 64:96]}
        for j, (p, h) in enumerate(fold_jobs):
            bufi = j % 2
            g, hp = h // 4, h % 4
            S.wait("tensor", lf[bufi], S.cnt[lf[bufi]])
            rhs = rhs_map[p]
            S.wait("tensor", "dve")
            for ch in range(NCH):
                S.emit("tensor", lambda E, b=bufi, c=ch, r=rhs: E.matmul(
                    out=PS[:, 512 * 6 + 32 * c: 512 * 6 + 32 * c + 32],
                    lhsT=wfold[b][:, 128 * c: 128 * (c + 1)],
                    rhs=r, start=True, stop=True, skip_group_check=True))
            S.tag("tensor", "pe")
            if p == 0:
                dst = Wqk[:].rearrange("q (c f) -> q c f", c=8)[:, :, g * 128 + 32 * hp: g * 128 + 32 * hp + 32]
            elif p == 1:
                dst = Wqk[:].rearrange("q (c f) -> q c f", c=8)[:, :, 512 + g * 128 + 32 * hp: 512 + g * 128 + 32 * hp + 32]
            else:
                dst = Wv[:].rearrange("q (c f) -> q c f", c=8)[:, :, 32 * h: 32 * h + 32]
            psrc = PS[:, 512 * 6: 512 * 6 + 256].rearrange("q (c f) -> q c f", c=8)
            S.wait("vector", "pe")
            S.emit("vector", lambda E, d=dst, s_=psrc: E.tensor_copy(out=d, in_=s_).then_inc(sems["dve"], 1))
            S.inc("dve")
            S.wait("tensor", "dve")
            S.emit("tensor", lambda E, pp=p, hh=h, r=rhs: E.matmul(
                out=PS[0:1, 512 * 6 + 288: 512 * 6 + 320], lhsT=bmat_s[:, pp * 16 + hh: pp * 16 + hh + 1],
                rhs=r, start=True, stop=True).then_inc(sems["pe"], 1))
            S.inc("pe")
            S.wait("vector", "pe")
            if p == 2:
                S.emit("vector", lambda E, hh=h: E.tensor_copy(
                    out=bv_rowf[0:1, 32 * hh: 32 * hh + 32],
                    in_=PS[0:1, 512 * 6 + 288: 512 * 6 + 320]).then_inc(sems["dve"], 1))
                S.inc("dve")
            else:
                S.emit("vector", lambda E: E.tensor_copy(
                    out=rowA[0:1, 0:32], in_=PS[0:1, 512 * 6 + 288: 512 * 6 + 320]).then_inc(sems["dve"], 1))
                S.inc("dve")
                S.wait("tensor", "dve")
                S.emit("tensor", lambda E: E.matmul(
                    out=PS[0:32, 512 * 6 + 320: 512 * 6 + 321], lhsT=rowA[0:1, 0:32],
                    rhs=ones_f[0:1, 0:1], start=True, stop=True).then_inc(sems["pe"], 1))
                S.inc("pe")
                S.wait("vector", "pe")
                col = g if p == 0 else 4 + g
                S.emit("vector", lambda E, c=col, hp_=hp: E.tensor_copy(
                    out=bqk_col[32 * hp_: 32 * hp_ + 32, c: c + 1],
                    in_=PS[0:32, 512 * 6 + 320: 512 * 6 + 321]).then_inc(sems["dve"], 1))
                S.inc("dve")
            S.wait("tensor", "dve")
            if j + 2 < len(fold_jobs):
                S.wait("sync", "pe")
                fold_load(j + 2)

        S.wait("scalar", "dve")
        S.emit("scalar", lambda E: E.copy(out=bv_row[:], in_=bv_rowf[:]).then_inc(sems["act"], 1))
        S.inc("act")

        lc = ("lcA", "lcB")

        def cp_load(h):
            S.dma("sync", cpslc[h % 2], cpw[64 * h: 64 * h + 64, :], lc[h % 2])

        NH_cp = 0 if stop_after in ("ph0", "fold1", "fold3", "foldq", "foldqkv") else NH
        if NH_cp:
            cp_load(0)
            cp_load(1)
        for h in range(NH_cp):
            bufi = h % 2
            g, hp = h // 4, h % 4
            S.wait("tensor", lc[bufi], S.cnt[lc[bufi]])
            for half in range(2):
                S.emit("tensor", lambda E, b=bufi, hf=half: E.matmul(
                    out=PS[0:32, 512 * hf: 512 * hf + 512], lhsT=wk_s[:, 96:128],
                    rhs=cpslc[b][:, 512 * hf: 512 * (hf + 1)], start=True, stop=True))
            S.tag("tensor", "pe")
            S.wait("vector", "pe")
            S.emit("vector", lambda E, g_=g, hp_=hp: E.tensor_copy(
                out=Cp[32 * hp_: 32 * hp_ + 32, g_ * 1024: g_ * 1024 + 1024],
                in_=PS[0:32, 0:1024]).then_inc(sems["dve"], 1))
            S.inc("dve")
            S.wait("tensor", "dve")
            if h + 2 < NH:
                S.wait("sync", "pe")
                cp_load(h + 2)

        stop_now = stop_after in ("fold", "ph0", "fold1", "fold3", "foldq", "foldqkv", "foldcp")
        lx = ("lxA", "lxB")
        xt_free_dve = {}
        s12_free = {}
        if stop_now:
            NQT_eff = 0
        else:
            NQT_eff = NQT

        def x_load(tb):
            if tb - 2 in xt_free_dve:
                S.wait("sync", "dve", xt_free_dve[tb - 2])
            S.dma("sync", xt[tb % 2][:], x_t[128 * tb: 128 * (tb + 1), :], lx[tb % 2])

        if not stop_now:
            S.wait("sync", "pe")
            S.wait("sync", "dve")
            x_load(0)
            x_load(1)
        for tile in range(NQT_eff):
            for j in range(4):
                tb = tile * 4 + j
                bufi = tb % 2
                S.wait("vector", lx[bufi], S.cnt[lx[bufi]])
                S.emit("vector", lambda E, b=bufi: E.reduce_sum(
                    out=s12[:, 0:1], in_=xt[b][:], axis=AX.X))
                S.tagwait("vector", "dve")
                S.wait("scalar", lx[bufi], S.cnt[lx[bufi]])
                S.wait("scalar", "dve", s12_free.get(tb - 1, 0))
                S.emit("scalar", lambda E, b=bufi: E.activation(
                    out=probsT[0][:, 1024:2048], in_=xt[b][:], func=AF.Square,
                    accum_out=s12[:, 4:5]).then_inc(sems["act"], 1))
                S.inc("act")
                S.wait("vector", "act")
                S.emit("vector", lambda E: E.tensor_scalar(
                    out=mcol[:], in0=s12[:, 0:1], scalar1=1.0 / D, scalar2=None, op0=ALU.mult))
                S.emit("vector", lambda E: E.tensor_scalar(
                    out=vcol[:], in0=s12[:, 4:5], scalar1=1.0 / D, scalar2=1e-5, op0=ALU.mult, op1=ALU.add))
                S.tagwait("vector", "dve")
                s12_free[tb] = S.cnt["dve"]
                S.emit("vector", lambda E: E.tensor_tensor(out=msq[:], in0=mcol[:], in1=mcol[:], op=ALU.mult))
                S.emit("vector", lambda E, b=bufi: E.tensor_scalar(
                    out=tmp_f[:], in0=xt[b][:], scalar1=mcol[:, 0:1], scalar2=None, op0=ALU.subtract))
                S.tagwait("vector", "dve")
                S.emit("vector", lambda E: E.tensor_tensor(
                    out=vcol[:], in0=vcol[:], in1=msq[:], op=ALU.subtract).then_inc(sems["dve"], 1))
                S.inc("dve")
                xt_free_dve[tb] = S.cnt["dve"]
                S.wait("scalar", "dve")
                S.emit("scalar", lambda E: E.activation(out=lncol[:], in_=vcol[:], func=AF.Ln))
                S.emit("scalar", lambda E: E.activation(
                    out=rstd[:], in_=lncol[:], func=AF.Exp, scale=-0.5).then_inc(sems["act"], 1))
                S.inc("act")
                S.wait("vector", "act")
                S.emit("vector", lambda E: E.tensor_scalar(
                    out=xn_t[:], in0=tmp_f[:], scalar1=rstd[:, 0:1], scalar2=None, op0=ALU.mult).then_inc(sems["dve"], 1))
                S.inc("dve")
                if tb == NTB - 1 and "d_xnt" in dbg_t:
                    S.emit("vector", lambda E: E.tensor_copy(out=smallA[:, 0:1], in_=mcol[:]))
                    S.emit("vector", lambda E: E.tensor_copy(out=smallA[:, 1:2], in_=rstd[:]))
                    S.emit("vector", lambda E: E.tensor_copy(out=smallA[:, 2:3], in_=s12[:, 0:1]))
                    S.emit("vector", lambda E: E.tensor_copy(out=smallA[:, 3:4], in_=s12[:, 4:5]).then_inc(sems["dve"], 1))
                    S.inc("dve")
                    S.wait("sync", "dve")
                    S.dma("sync", dbg_t["d_xnt"], xn_t[:], "st")
                    S.dma("sync", dbg_t["d_cols"], smallA[:, 0:8], "st")
                if tb + 2 < NTB:
                    x_load(tb + 2)
                S.wait("tensor", "dve")
                for ch in range(NCH):
                    S.emit("tensor", lambda E, c=ch: E.transpose(
                        out=PSB[:, 128 * c: 128 * (c + 1)], in_=xn_t[:, 128 * c: 128 * (c + 1)],
                        identity=identbf[:]))
                S.tag("tensor", "pe")
                S.wait("vector", "pe")
                xnf_dst = xn_f[:].rearrange("p (c q t) -> p c q t", c=8, q=4, t=128)[:, :, j, :]
                psb_src = PSB[:].rearrange("p (c t) -> p c t", c=8)
                S.emit("vector", lambda E, d=xnf_dst, s=psb_src: E.tensor_copy(out=d, in_=s).then_inc(sems["dve"], 1))
                S.inc("dve")
                S.wait("tensor", "dve")
                for ch in range(NCH):
                    S.emit("tensor", lambda E, c=ch, jj=j: E.matmul(
                        out=bank(2), lhsT=xn_f[:, c * 512 + 128 * jj: c * 512 + 128 * jj + 128],
                        rhs=Wv[:, c * 512: (c + 1) * 512], start=(c == 0), stop=False))
                S.emit("tensor", lambda E: E.matmul(
                    out=bank(2), lhsT=ones_b[0:1, :], rhs=bv_row[0:1, :],
                    start=False, stop=True).then_inc(sems["pe"], 1))
                S.inc("pe")
                S.wait("vector", "pe")
                vt_dst = vT[:, 528 * tb: 528 * (tb + 1)].rearrange("p (h l) -> p h l", h=16, l=33)[:, :, 0:32]
                pv_src = bank(2).rearrange("p (h l) -> p h l", h=16, l=32)
                S.emit("vector", lambda E, d=vt_dst, s=pv_src: E.tensor_copy(out=d, in_=s).then_inc(sems["dve"], 1))
                S.inc("dve")
                S.wait("tensor", "dve")

            tilebase = tile * 512
            for ft in range(8):
                for ch in range(NCH):
                    S.emit("tensor", lambda E, f=ft, c=ch: E.matmul(
                        out=bank(3), lhsT=Wqk[:, c * 1024 + f * 128: c * 1024 + f * 128 + 128],
                        rhs=xn_f[:, c * 512: (c + 1) * 512], start=(c == 0), stop=(c == NCH - 1)))
                S.tag("tensor", "pe")
                dst = (qp[ft] if ft < 4 else kp[ft - 4])[:, tilebase: tilebase + 512]
                S.wait("scalar", "pe")
                S.emit("scalar", lambda E, d=dst, f=ft: E.activation(
                    out=d, in_=bank(3), func=AF.Identity, bias=bqk_col[:, f: f + 1], scale=1.0).then_inc(sems["act"], 1))
                S.inc("act")
                S.wait("tensor", "act")

        ph1_pe, ph1_dve, ph1_act = S.cnt["pe"], S.cnt["dve"], S.cnt["act"]

        if "d_qp" in dbg_t:
            S.wait("sync", "act", ph1_act)
            for g in range(4):
                S.dma("sync", dbg_t["d_qp"][:, g * T:(g + 1) * T], qp[g][:], "st")
        if "d_kp" in dbg_t:
            S.wait("sync", "act", ph1_act)
            for g in range(4):
                S.dma("sync", dbg_t["d_kp"][:, g * T:(g + 1) * T], kp[g][:], "st")
        if "d_vt" in dbg_t:
            S.wait("sync", "dve", ph1_dve)
            S.dma("sync", dbg_t["d_vt"], vT[:], "st")
        if "d_xn" in dbg_t:
            S.wait("sync", "dve", ph1_dve)
            S.dma("sync", dbg_t["d_xn"], xn_f[:], "st")

        stop_now = stop_now or stop_after == "ph1"
        sqb = xn_f
        for side, grp in ([] if stop_now else [("q", qp), ("k", kp)]):
            for g in range(4):
                S.wait("vector", "act", ph1_act)
                S.emit("vector", lambda E, t_=grp[g]: E.tensor_tensor(
                    out=sqb[:], in0=t_[:], in1=t_[:], op=ALU.mult).then_inc(sems["dve"], 1))
                S.inc("dve")
                S.wait("tensor", "dve")
                col = (0 if side == "q" else 4) + g
                for nt in range(8):
                    S.emit("tensor", lambda E, n=nt: E.matmul(
                        out=PS[0:4, 512 * 4: 512 * 4 + 512], lhsT=eg[:], rhs=sqb[:, 512 * n: 512 * (n + 1)],
                        start=True, stop=True).then_inc(sems["pe"], 1))
                    S.inc("pe")
                    S.wait("vector", "pe")
                    S.emit("vector", lambda E, c=col, n=nt: E.reduce_max(
                        out=gmax2[0:4, 8 * c + n: 8 * c + n + 1],
                        in_=PS[0:4, 512 * 4: 512 * 4 + 512], axis=AX.X).then_inc(sems["dve"], 1))
                    S.inc("dve")
                    S.wait("tensor", "dve")
        if not stop_now:
            S.emit("vector", lambda E: E.memset(smallA[0:2, 10:11], 1.0))
            S.tagwait("vector", "dve")
            S.emit("vector", lambda E: E.reduce_max(out=smallA[0:4, 1:2], in_=gmax2[0:4, 0:32], axis=AX.X))
            S.emit("vector", lambda E: E.reduce_max(out=smallA[0:4, 2:3], in_=gmax2[0:4, 32:64], axis=AX.X).then_inc(sems["dve"], 1))
            S.inc("dve")
            S.wait("tensor", "dve")
            S.emit("tensor", lambda E: E.transpose(
                out=PS[0:2, 512 * 4: 512 * 4 + 4], in_=smallA[0:4, 1:3],
                identity=identf[0:4, 0:4]).then_inc(sems["pe"], 1))
            S.inc("pe")
            S.wait("vector", "pe")
            S.emit("vector", lambda E: E.tensor_copy(out=smallA[0:2, 4:8], in_=PS[0:2, 512 * 4: 512 * 4 + 4]))
            S.tagwait("vector", "dve")
            S.emit("vector", lambda E: E.reduce_max(out=smallA[0:2, 8:9], in_=smallA[0:2, 4:8], axis=AX.X).then_inc(sems["dve"], 1))
            S.inc("dve")
            S.wait("scalar", "dve")
            S.emit("scalar", lambda E: E.activation(out=smallA[0:2, 9:10], in_=smallA[0:2, 8:9], func=AF.Ln).then_inc(sems["act"], 1))
            S.inc("act")
            S.wait("tensor", "act")
            S.emit("tensor", lambda E: E.matmul(
                out=PS[0:1, 512 * 4 + 16: 512 * 4 + 17], lhsT=smallA[0:2, 9:10], rhs=smallA[0:2, 10:11],
                start=True, stop=True).then_inc(sems["pe"], 1))
            S.inc("pe")
            S.wait("vector", "pe")
            S.emit("vector", lambda E: E.tensor_copy(out=rowA[0:1, 24:25], in_=PS[0:1, 512 * 4 + 16: 512 * 4 + 17]).then_inc(sems["dve"], 1))
            S.inc("dve")
            S.wait("scalar", "dve")
            S.emit("scalar", lambda E: E.activation(
                out=rowA[0:1, 26:27], in_=rowA[0:1, 24:25], func=AF.Exp, scale=0.5).then_inc(sems["act"], 1))
            S.inc("act")
            S.wait("vector", "act")
            S.emit("vector", lambda E: E.tensor_scalar(
                out=rowA[0:1, 28:29], in0=rowA[0:1, 26:27], scalar1=80.0, scalar2=None, op0=ALU.min))
            S.tagwait("vector", "dve")
            S.emit("vector", lambda E: E.tensor_scalar(
                out=rowA[0:1, 30:31], in0=rowA[0:1, 28:29], scalar1=-1.0, scalar2=None, op0=ALU.mult).then_inc(sems["dve"], 1))
            S.inc("dve")
            S.wait("tensor", "dve")
            S.emit("tensor", lambda E: E.matmul(
                out=PS[:, 512 * 4 + 8: 512 * 4 + 9], lhsT=ones_f[0:1, :], rhs=rowA[0:1, 30:31],
                start=True, stop=True).then_inc(sems["pe"], 1))
            S.inc("pe")
            S.wait("vector", "pe")
            S.emit("vector", lambda E: E.tensor_copy(out=negc_col[:], in_=PS[:, 512 * 4 + 8: 512 * 4 + 9]).then_inc(sems["dve"], 1))
            S.inc("dve")
            S.wait("scalar", "dve")
            if "d_negc" in dbg_t:
                S.wait("sync", "dve")
                S.dma("sync", dbg_t["d_negc"], negc_col[:], "st")

        stop_now = stop_now or stop_after == "ph15"
        def pv_slice(hp, s):
            idx = hp * 4 + s
            return PS[:, 512 * 4 + 33 * idx: 512 * 4 + 33 * idx + 33]

        lr = ("lrA", "lrB")
        xr_free_dve = {}

        def xr_load(tb):
            if tb - 2 in xr_free_dve:
                S.wait("sync", "dve", xr_free_dve[tb - 2])
            S.dma("sync", xr[tb % 2][:], x_t[128 * tb: 128 * (tb + 1), :], lr[tb % 2])

        if not stop_now:
            S.wait("sync", "pe")
            S.wait("sync", "dve")
            xr_load(0)
            xr_load(1)
        qt_list = [] if stop_now else (range(1) if stop_after == "ph2qt0" else range(NQT))
        def at2af(qtm, rnd):
            aT = aTb[qtm % 2]
            S.wait("tensor", "dve")
            for k in range(8):
                s, lcg = (rnd * 8 + k) // 4, (rnd * 8 + k) % 4
                S.emit("tensor", lambda E, s_=s, l_=lcg, k_=k, aT_=aT: E.transpose(
                    out=PSB[:, 128 * k_: 128 * (k_ + 1)],
                    in_=aT_[:, 512 * s_ + 128 * l_: 512 * s_ + 128 * l_ + 128],
                    identity=identbf[:]))
            S.tag("tensor", "pe")
            S.wait("vector", "pe")
            for k in range(8):
                s, lcg = (rnd * 8 + k) // 4, (rnd * 8 + k) % 4
                S.emit("vector", lambda E, s_=s, l_=lcg, k_=k: E.tensor_copy(
                    out=aF[:, 512 * l_ + 128 * s_: 512 * l_ + 128 * s_ + 128],
                    in_=PSB[:, 128 * k_: 128 * (k_ + 1)]))
            S.emit("vector", lambda E: E.engine_nop().then_inc(sems["dve"], 1))
            S.inc("dve")

        CPJ = PS[:, 2560: 3584]
        RTR = PS[:, 2560: 2568]

        def ph3_block(tb):
            s = tb % 4
            bufi = tb % 2
            S.wait("tensor", "dve")
            for half in range(2):
                for lcg in range(4):
                    S.emit("tensor", lambda E, l_=lcg, s_=s, hf=half: E.matmul(
                        out=CPJ[:, 512 * hf: 512 * hf + 512],
                        lhsT=aF[:, 512 * l_ + 128 * s_: 512 * l_ + 128 * s_ + 128],
                        rhs=Cp[:, l_ * 1024 + hf * 512: l_ * 1024 + hf * 512 + 512],
                        start=(l_ == 0), stop=False))
                S.emit("tensor", lambda E, hf=half: E.matmul(
                    out=CPJ[:, 512 * hf: 512 * hf + 512], lhsT=ones_b[0:1, :],
                    rhs=cpb_row[0:1, hf * 512: hf * 512 + 512], start=False, stop=True))
            S.tag("tensor", "pe")
            S.wait("vector", "pe")
            S.wait("vector", lr[bufi], S.cnt[lr[bufi]])
            S.wait("vector", "sx2", S.cnt["sx2"])
            S.emit("vector", lambda E, b=bufi: E.tensor_tensor(
                out=x2_t[:], in0=CPJ[:], in1=xr[b][:], op=ALU.add).then_inc(sems["dve"], 1))
            S.inc("dve")
            xr_free_dve[tb] = S.cnt["dve"]
            x2_ready = S.cnt["dve"]
            if tb + 2 < NTB:
                xr_load(tb + 2)
            S.wait("sync", "dve")
            S.dma("sync", x2s[128 * tb: 128 * (tb + 1), :], x2_t[:], "sx2")
            S.emit("vector", lambda E: E.reduce_sum(out=s12[:, 0:1], in_=x2_t[:], axis=AX.X))
            S.tagwait("vector", "dve")
            S.wait("scalar", "dve", x2_ready)
            S.emit("scalar", lambda E: E.activation(
                out=probsT[0][:, 1024:2048], in_=x2_t[:], func=AF.Square, accum_out=s12[:, 4:5]).then_inc(sems["act"], 1))
            S.inc("act")
            S.wait("vector", "act")
            S.emit("vector", lambda E: E.tensor_scalar(
                out=mcol[:], in0=s12[:, 0:1], scalar1=1.0 / D, scalar2=None, op0=ALU.mult))
            S.emit("vector", lambda E: E.tensor_scalar(
                out=vcol[:], in0=s12[:, 4:5], scalar1=1.0 / D, scalar2=1e-5, op0=ALU.mult, op1=ALU.add))
            S.tagwait("vector", "dve")
            S.emit("vector", lambda E: E.tensor_tensor(out=msq[:], in0=mcol[:], in1=mcol[:], op=ALU.mult))
            S.emit("vector", lambda E: E.tensor_scalar(
                out=tmp_f[:], in0=x2_t[:], scalar1=mcol[:, 0:1], scalar2=None, op0=ALU.subtract))
            S.tagwait("vector", "dve")
            S.emit("vector", lambda E: E.tensor_tensor(
                out=vcol[:], in0=vcol[:], in1=msq[:], op=ALU.subtract).then_inc(sems["dve"], 1))
            S.inc("dve")
            S.wait("scalar", "dve")
            S.emit("scalar", lambda E: E.activation(out=lncol[:], in_=vcol[:], func=AF.Ln))
            S.emit("scalar", lambda E: E.activation(
                out=rstd[:], in_=lncol[:], func=AF.Exp, scale=-0.5).then_inc(sems["act"], 1))
            S.inc("act")
            S.wait("vector", "act")
            S.emit("vector", lambda E: E.tensor_scalar(
                out=xn2_t[:], in0=tmp_f[:], scalar1=rstd[:, 0:1], scalar2=None, op0=ALU.mult).then_inc(sems["dve"], 1))
            S.inc("dve")
            S.wait("tensor", "dve")
            for ch in range(NCH):
                S.emit("tensor", lambda E, c=ch: E.transpose(
                    out=PSB[:, 128 * c: 128 * (c + 1)], in_=xn2_t[:, 128 * c: 128 * (c + 1)],
                    identity=identbf[:]))
            S.tag("tensor", "pe")
            S.wait("vector", "pe")
            S.wait("vector", "sxn", S.cnt["sxn"])
            S.emit("vector", lambda E: E.tensor_copy(out=xn2_f[:], in_=PSB[:]).then_inc(sems["dve"], 1))
            S.inc("dve")
            S.wait("sync", "dve")
            S.dma("sync", xn2s[tb], xn2_f[:], "sxn")
            S.wait("tensor", "dve")
            S.wait("tensor", "gio", 32)
            for ch in range(NCH):
                S.emit("tensor", lambda E, c=ch: E.matmul(
                    out=RTR, lhsT=xn2_f[:, 128 * c: 128 * (c + 1)],
                    rhs=rW[:, 8 * c: 8 * (c + 1)], start=(c == 0), stop=False))
            S.emit("tensor", lambda E: E.matmul(
                out=RTR, lhsT=ones_b[0:1, :], rhs=rb_row[0:1, :],
                start=False, stop=True).then_inc(sems["pe"], 1))
            S.inc("pe")
            S.wait("scalar", "pe")
            S.emit("scalar", lambda E: E.activation(
                out=pr8[:], in_=RTR, func=AF.Exp, scale=-1.0).then_inc(sems["act"], 1))
            S.inc("act")
            S.wait("vector", "act")
            S.emit("vector", lambda E: E.tensor_scalar(
                out=top_a[:], in0=pr8[:], scalar1=1.0, scalar2=None, op0=ALU.add))
            S.tagwait("vector", "dve")
            S.emit("vector", lambda E: E.reciprocal(out=pr8b[:], in_=top_a[:]))
            S.tagwait("vector", "dve")
            S.emit("vector", lambda E: E.reduce_max(out=mcol[:], in_=pr8b[:], axis=AX.X))
            S.tagwait("vector", "dve")
            S.emit("vector", lambda E: E.tensor_scalar(
                out=top_a[:], in0=pr8b[:], scalar1=mcol[:, 0:1], scalar2=None, op0=ALU.is_ge))
            S.tagwait("vector", "dve")
            S.emit("vector", lambda E: E.scalar_tensor_tensor(
                out=top_b[:], in0=top_a[:], scalar=-2.0, in1=pr8b[:], op0=ALU.mult, op1=ALU.add))
            S.tagwait("vector", "dve")
            S.emit("vector", lambda E: E.reduce_max(out=msq[:], in_=top_b[:], axis=AX.X))
            S.tagwait("vector", "dve")
            S.emit("vector", lambda E: E.tensor_scalar(
                out=top_a[:], in0=pr8b[:], scalar1=msq[:, 0:1], scalar2=None, op0=ALU.is_ge))
            S.tagwait("vector", "dve")
            S.emit("vector", lambda E: E.tensor_tensor(out=top_b[:], in0=pr8b[:], in1=top_a[:], op=ALU.mult))
            S.tagwait("vector", "dve")
            S.emit("vector", lambda E: E.reduce_sum(out=vcol[:], in_=top_b[:], axis=AX.X))
            S.tagwait("vector", "dve")
            S.emit("vector", lambda E: E.reciprocal(out=lncol[:], in_=vcol[:]))
            S.tagwait("vector", "dve")
            S.emit("vector", lambda E, t_=tb: E.tensor_scalar(
                out=w_tok[:, 8 * t_: 8 * (t_ + 1)], in0=top_b[:],
                scalar1=lncol[:, 0:1], scalar2=None, op0=ALU.mult).then_inc(sems["dve"], 1))
            S.inc("dve")

        exp_inc = {0: 0, 1: 0}
        evac_inc = {0: 0, 1: 0}
        pv_gate = {0: 0, 1: 0}
        qt_seq = list(qt_list)
        tail = [qt_seq[-1] + 1] if qt_seq else []
        for qtx in qt_seq + tail:
            run_attn = qtx in qt_seq
            nkc = 4 * qtx + 4
            aT = aTb[qtx % 2]
            for g2 in (range(8) if run_attn else []):
                if qtx >= 1:
                    if g2 == 0:
                        at2af(qtx - 1, 0)
                    elif g2 == 1:
                        at2af(qtx - 1, 1)
                    elif g2 in (2, 3, 4, 5):
                        ph3_block(4 * (qtx - 1) + (g2 - 2))
                gq = g2 // 2
                po = 64 * (g2 % 2)

                def emit_pack(kc):
                    jdiag = kc - 4 * qtx
                    p = kc % 2
                    base = 1024 * p
                    S.wait("tensor", "act", exp_inc[p])
                    for hp in range(2):
                        trim = 128 * jdiag if jdiag > 0 else 0
                        row = po + 32 * hp
                        S.emit("tensor", lambda E, r=row, h_=hp, k_=kc, tr=trim,
                               q0=qtx * 512, b0=base, gq_=gq: E.matmul(
                            out=PS[:, b0 + 512 * h_ + tr: b0 + 512 * h_ + 512],
                            lhsT=kp[gq_][r: r + 32, 128 * k_: 128 * k_ + 128],
                            rhs=qp[gq_][r: r + 32, q0 + tr: q0 + 512],
                            start=True, stop=True, tile_position=(r, 0)))
                    S.tag("tensor", "pe")
                    pe_pack = S.cnt["pe"]
                    if jdiag >= 0:
                        S.wait("vector", "pe", pe_pack)
                        for hp in range(2):
                            off = base + 512 * hp + 128 * jdiag
                            S.emit("vector", lambda E, o=off: E.tensor_tensor(
                                out=PS[:, o: o + 128], in0=PS[:, o: o + 128], in1=trimask[:], op=ALU.add))
                        S.emit("vector", lambda E: E.engine_nop().then_inc(sems["dve"], 1))
                        mv = S.inc("dve")
                        S.wait("scalar", "dve", mv)
                    S.wait("scalar", "pe", pe_pack)
                    S.emit("scalar", lambda E, b=p: E.activation(
                        out=probsT[b][:, 0:1024], in_=PS[:, 1024 * b: 1024 * b + 1024], func=AF.Exp,
                        bias=negc_col[:, 0:1], scale=1.0).then_inc(sems["act"], 1))
                    exp_inc[p] = S.inc("act")

                def emit_pv(kc):
                    p = kc % 2
                    S.wait("tensor", "act", exp_inc[p])
                    pv_gate[p] = None
                    for hp in range(2):
                        h = 2 * g2 + hp
                        for s in range(4):
                            if kc > 4 * qtx + s:
                                continue
                            idx = hp * 4 + s
                            S.emit("tensor", lambda E, b=p, h_=hp, hh=h, s_=s, k_=kc,
                                   st_=(kc == 0 and idx == 0), sp=(kc == 4 * qtx + s): E.matmul(
                                out=pv_slice(h_, s_),
                                lhsT=probsT[b][:, 512 * h_ + 128 * s_: 512 * h_ + 128 * s_ + 128],
                                rhs=vT[:, 528 * k_ + 33 * hh: 528 * k_ + 33 * hh + 33],
                                start=st_, stop=sp, skip_group_check=True))
                    S.tag("tensor", "pe")
                    pv_gate[p] = S.cnt["pe"]

                for kc in range(nkc):
                    emit_pack(kc)
                    if kc >= 1:
                        emit_pv(kc - 1)
                emit_pv(nkc - 1)
                S.wait("vector", "pe")
                for hp in range(2):
                    for s in range(4):
                        S.emit("vector", lambda E, h_=hp, s_=s: E.reciprocal(
                            out=rlcol[:, h_ * 4 + s_: h_ * 4 + s_ + 1], in_=pv_slice(h_, s_)[:, 32:33]))
                S.tagwait("vector", "dve")
                for hp in range(2):
                    h = 2 * g2 + hp
                    for s in range(4):
                        S.emit("vector", lambda E, h_=hp, hh=h, s_=s, aT_=aT: E.tensor_scalar(
                            out=aT_[:, 512 * s_ + 32 * hh: 512 * s_ + 32 * hh + 32],
                            in0=pv_slice(h_, s_)[:, 0:32],
                            scalar1=rlcol[:, h_ * 4 + s_: h_ * 4 + s_ + 1], scalar2=None, op0=ALU.mult))
                S.emit("vector", lambda E: E.engine_nop().then_inc(sems["dve"], 1))
                S.inc("dve")
                S.wait("tensor", "dve")

            if not run_attn and qtx >= 1:
                if "d_at" in dbg_t and qtx - 1 == 0:
                    S.wait("sync", "dve")
                    for s in range(4):
                        S.dma("sync", dbg_t["d_at"][s], aTb[(qtx - 1) % 2][:, 512 * s: 512 * (s + 1)], "st")
                at2af(qtx - 1, 0)
                at2af(qtx - 1, 1)
                if "d_af" in dbg_t and qtx - 1 == 0:
                    S.wait("sync", "dve")
                    S.dma("sync", dbg_t["d_af"], aF[:], "st")
                for s in range(4):
                    ph3_block(4 * (qtx - 1) + s)

        ph3_pe, ph3_dve, ph3_act = S.cnt["pe"], S.cnt["dve"], S.cnt["act"]
        ph3_sx2, ph3_sxn = S.cnt["sx2"], S.cnt["sxn"]

        if "d_x2" in dbg_t:
            S.wait("sync", "sx2", ph3_sx2)
            S.dma("sync", dbg_t["d_x2"], x2s, "st")
        if "d_w" in dbg_t:
            S.wait("sync", "dve", ph3_dve)
            S.dma("sync", dbg_t["d_w"], w_tok[:], "st")
        if "d_xn2" in dbg_t:
            S.wait("sync", "sxn", ph3_sxn)
            for tb in range(NTB):
                S.dma("sync", dbg_t["d_xn2"][tb], xn2s[tb], "st")

        st12.close()
        stop_now = stop_now or stop_after in ("ph2qt0", "ph3")

        st4 = ExitStack()
        xn2q = SB(st4, "xn2q", [128, 8 * 1024], BF16)
        acc = SB(st4, "acc", [128, 8 * 1024], F32)
        wG = [SB(st4, "gbufA", [128, 8 * 1024], BF16), SB(st4, "gbufB", [128, 8 * 1024], BF16)]
        wU = [SB(st4, "ubufA", [128, 8 * 1024], BF16), SB(st4, "ubufB", [128, 8 * 1024], BF16)]
        wD = [SB(st4, "dbufA", [128, 8 * 1024], BF16), SB(st4, "dbufB", [128, 8 * 1024], BF16)]
        gsb = SB(st4, "gsb", [128, 8 * 512], BF16)
        hsb = SB(st4, "hsb", [128, 8 * 512], BF16)

        def gate_src(e, ih):
            return (sgT if e < 0 else gw[e])[:, ih * 1024: (ih + 1) * 1024]

        def up_src(e, ih):
            return (suT if e < 0 else uw[e])[:, ih * 1024: (ih + 1) * 1024]

        def down_src(e, ih):
            return (sdT if e < 0 else dw[e])[ih * 1024: (ih + 1) * 1024, :]

        wsem = {0: ("wgA", "wuA", "wdA"), 1: ("wgB", "wuB", "wdB")}
        pe_free_at = {0: 0, 1: 0}
        silu_inc = [0, 0, 0, 0]
        htt_inc = [0, 0, 0, 0]
        pair_dve = [0, 0]
        dbank_free = {4: 0, 5: 0, 6: 0}
        dctr = [0]
        EXPERTS = [-1] + list(range(E))
        halves = [(e, ih) for e in EXPERTS for ih in range(2)]

        S.wait("gpsimd", "pe", ph3_pe)
        S.wait("gpsimd", "dve", ph3_dve)
        S.wait("gpsimd", "act", ph3_act)
        S.wait("sync", "pe", ph3_pe)
        S.wait("sync", "dve", ph3_dve)
        S.wait("sync", "act", ph3_act)

        def load_half(e, ih, slot):
            sg_, su_, sd_ = wsem[slot]
            for ch in range(NCH):
                S.dma("gpsimd", wG[slot][:, ch * 1024: (ch + 1) * 1024],
                      gate_src(e, ih)[128 * ch: 128 * (ch + 1), :], sg_)
            for ch in range(NCH):
                S.dma("gpsimd", wU[slot][:, ch * 1024: (ch + 1) * 1024],
                      up_src(e, ih)[128 * ch: 128 * (ch + 1), :], su_)
            for ch in range(NCH):
                S.dma("gpsimd", wD[slot][:, ch * 1024: (ch + 1) * 1024],
                      down_src(e, ih)[128 * ch: 128 * (ch + 1), :], sd_)

        for Qi in ([] if stop_now else range(NQ)):
            S.wait("sync", "sx2", ph3_sx2)
            S.wait("sync", "sxn", ph3_sxn)
            if Qi > 0:
                S.wait("sync", "pe", S.cnt["pe"])
                S.wait("sync", "dve", S.cnt["dve"])
                S.wait("sync", "st", S.cnt["st"])
            for k in range(8):
                tb = Qi * 8 + k
                src = xn2s[tb].rearrange("p (c t) -> p c t", c=8)
                dstq = xn2q[:].rearrange("p (c q t) -> p c q t", c=8, q=8, t=128)[:, :, k, :]
                S.dma("sync", dstq, src, "lq")
                S.dma("sync", acc[:, k * 1024: (k + 1) * 1024], x2s[128 * tb: 128 * (tb + 1), :], "lq")

            for hi, (e, ih) in enumerate(halves):
                slot = hi % 2
                sg_, su_, sd_ = wsem[slot]
                S.wait("gpsimd", "pe", pe_free_at[slot])
                load_half(e, ih, slot)
                S.wait("tensor", "lq", S.cnt["lq"])
                for tt in range(2):
                    for pr in range(4):
                        pb0 = 0 if pr % 2 == 0 else 2
                        S.wait("tensor", sg_, S.cnt[sg_])
                        S.wait("tensor", "dve", pair_dve[pr % 2])
                        for it in (2 * pr, 2 * pr + 1):
                            bnk = pb0 + (it % 2)
                            for ch in range(NCH):
                                S.emit("tensor", lambda E, b=bnk, i_=it, c=ch, t_=tt, sl=slot: E.matmul(
                                    out=bank(b), lhsT=wG[sl][:, c * 1024 + 128 * i_: c * 1024 + 128 * i_ + 128],
                                    rhs=xn2q[:, c * 1024 + 512 * t_: c * 1024 + 512 * t_ + 512],
                                    start=(c == 0), stop=(c == NCH - 1)))
                        S.tag("tensor", "pe")
                        S.wait("scalar", "pe")
                        S.wait("scalar", "dve", htt_inc[pr])
                        S.emit("scalar", lambda E, b=pb0, p_=pr: E.activation(
                            out=gsb[:, 1024 * p_: 1024 * (p_ + 1)], in_=PS[:, 512 * b: 512 * b + 1024],
                            func=AF.Silu).then_inc(sems["act"], 1))
                        silu_inc[pr] = S.inc("act")
                    for pr in range(4):
                        pb0 = 0 if pr % 2 == 0 else 2
                        S.wait("tensor", su_, S.cnt[su_])
                        S.wait("tensor", "act", silu_inc[pr])
                        for it in (2 * pr, 2 * pr + 1):
                            bnk = pb0 + (it % 2)
                            for ch in range(NCH):
                                S.emit("tensor", lambda E, b=bnk, i_=it, c=ch, t_=tt, sl=slot: E.matmul(
                                    out=bank(b), lhsT=wU[sl][:, c * 1024 + 128 * i_: c * 1024 + 128 * i_ + 128],
                                    rhs=xn2q[:, c * 1024 + 512 * t_: c * 1024 + 512 * t_ + 512],
                                    start=(c == 0), stop=(c == NCH - 1)))
                        S.tag("tensor", "pe")
                        S.wait("vector", "pe")
                        S.wait("vector", "act", silu_inc[pr])
                        S.emit("vector", lambda E, b=pb0, p_=pr: E.tensor_tensor(
                            out=hsb[:, 1024 * p_: 1024 * (p_ + 1)], in0=gsb[:, 1024 * p_: 1024 * (p_ + 1)],
                            in1=PS[:, 512 * b: 512 * b + 1024], op=ALU.mult).then_inc(sems["dve"], 1))
                        htt_inc[pr] = S.inc("dve")
                        pair_dve[pr % 2] = htt_inc[pr]
                    S.wait("tensor", sd_, S.cnt[sd_])
                    S.wait("tensor", "dve", htt_inc[3])
                    for j in range(4):
                        tbq = tt * 4 + j
                        tbg = Qi * 8 + tbq
                        bks = []
                        for dh in range(2):
                            b = 4 + (dctr[0] % 3)
                            dctr[0] += 1
                            bks.append(b)
                            S.wait("tensor", "dve", dbank_free[b])
                            for ic in range(NCH):
                                co = (ic // 2) * 1024 + (ic % 2) * 512 + 128 * j
                                S.emit("tensor", lambda E, b_=b, c=ic, co_=co, d_=dh, sl=slot: E.matmul(
                                    out=bank(b_), lhsT=hsb[:, co_: co_ + 128],
                                    rhs=wD[sl][:, c * 1024 + d_ * 512: c * 1024 + d_ * 512 + 512],
                                    start=(c == 0), stop=(c == NCH - 1)))
                        S.tag("tensor", "pe")
                        S.wait("vector", "pe")
                        for dh in range(2):
                            b = bks[dh]
                            aslice = acc[:, 1024 * tbq + 512 * dh: 1024 * tbq + 512 * dh + 512]
                            if e < 0:
                                S.emit("vector", lambda E, b_=b, a_=aslice: E.tensor_tensor(
                                    out=a_, in0=bank(b_), in1=a_, op=ALU.add).then_inc(sems["dve"], 1))
                            else:
                                S.emit("vector", lambda E, b_=b, a_=aslice, e_=e, tg=tbg: E.scalar_tensor_tensor(
                                    out=a_, in0=bank(b_), scalar=w_tok[:, 8 * tg + e_: 8 * tg + e_ + 1],
                                    in1=a_, op0=ALU.mult, op1=ALU.add).then_inc(sems["dve"], 1))
                            dbank_free[b] = S.inc("dve")
                pe_free_at[slot] = S.cnt["pe"]

            S.wait("sync", "dve", S.cnt["dve"])
            for k in range(8):
                tb = Qi * 8 + k
                S.dma("sync", out[128 * tb: 128 * (tb + 1), :], acc[:, k * 1024: (k + 1) * 1024], "st")

        st4.close()

        S.wait("sync", "st", S.cnt["st"])
        S.wait("gpsimd", "st", S.cnt["st"])

        with nc.Block() as block:
            @block.sync
            def _(e):
                for f in S.q["sync"]:
                    f(e)

            @block.gpsimd
            def _(e):
                for f in S.q["gpsimd"]:
                    f(e)

            @block.vector
            def _(e):
                for f in S.q["vector"]:
                    f(e)

            @block.scalar
            def _(e):
                for f in S.q["scalar"]:
                    f(e)

            @block.tensor
            def _(e):
                for f in S.q["tensor"]:
                    f(e)

    return nc


def _host_inputs(inputs):
    f32 = np.float32
    x = np.asarray(inputs["L0_attn_hidden_states"], f32)
    ca = np.asarray(inputs["L0_attn_c_attn_w"], f32)
    cab = np.asarray(inputs["L0_attn_c_attn_b"], f32)
    cpw_ = np.asarray(inputs["L0_attn_c_proj_w"], f32)
    cpb_ = np.asarray(inputs["L0_attn_c_proj_b"], f32)
    wk_c = np.asarray(inputs["L0_attn_wk_c"], f32)
    wk_e = np.asarray(inputs["L0_attn_wk_e"], f32)
    wv_c = np.asarray(inputs["L0_attn_wv_c"], f32)
    wv_e = np.asarray(inputs["L0_attn_wv_e"], f32)
    sg = np.asarray(inputs["L0_ffn_shared_gate"], f32)
    su = np.asarray(inputs["L0_ffn_shared_up"], f32)
    sd = np.asarray(inputs["L0_ffn_shared_down"], f32)
    gwv = np.asarray(inputs["L0_ffn_gate_w"], f32)
    uwv = np.asarray(inputs["L0_ffn_up_w"], f32)
    dwv = np.asarray(inputs["L0_ffn_down_w"], f32)
    rw = np.asarray(inputs["L0_ffn_router_w"], f32)
    rbv = np.asarray(inputs["L0_ffn_routing_bias"], f32)

    shared = {
        "caT": np.ascontiguousarray(ca.T),
        "b_mat": np.ascontiguousarray(cab.reshape(3, 16, 64).transpose(2, 0, 1).reshape(64, 48)),
        "wkeTh": np.ascontiguousarray(wk_e.T),
        "wkch": np.ascontiguousarray(wk_c),
        "wvch": np.ascontiguousarray(wv_c),
        "wveTh": np.ascontiguousarray(wv_e.T),
        "cpw": np.ascontiguousarray(cpw_),
        "cpb": np.ascontiguousarray(cpb_.reshape(1, D)),
        "rw_pk": np.ascontiguousarray(rw.T.reshape(8, 128, 8).transpose(1, 0, 2).reshape(128, 64)),
        "rb": np.ascontiguousarray(rbv.reshape(1, E)),
        "sgT": np.ascontiguousarray(sg.T),
        "suT": np.ascontiguousarray(su.T),
        "sdT": np.ascontiguousarray(sd.T),
        "gw": np.ascontiguousarray(gwv),
        "uw": np.ascontiguousarray(uwv),
        "dw": np.ascontiguousarray(dwv),
        "identbf": np.eye(128, dtype=BF16NP),
        "identf": np.eye(128, dtype=f32),
        "trimask": np.where(np.arange(128)[:, None] > np.arange(128)[None, :],
                            f32(-1e30), f32(0.0)).astype(f32),
        "eg": (np.arange(128)[:, None] // 32 == np.arange(4)[None, :]).astype(BF16NP),
    }
    in_maps = []
    for c in range(8):
        m = dict(shared)
        m["x_t"] = np.ascontiguousarray(x[c])
        in_maps.append(m)
    return in_maps


def kernel(**inputs):
    in_maps = _host_inputs(inputs)
    nc = build_program()
    res = run_bass_kernel_spmd(nc, in_maps, core_ids=list(range(8)))
    return np.stack([r["out"] for r in res.results], axis=0)
